# version 13
# baseline (speedup 1.0000x reference)
import numpy as np
from contextlib import ExitStack
import ml_dtypes
import concourse.bass as bass
import concourse.mybir as mybir
from concourse.bass_utils import run_bass_kernel_spmd
from concourse.alu_op_type import AluOpType as OP

F32 = mybir.dt.float32
BF16 = mybir.dt.bfloat16
AFT = mybir.ActivationFunctionType
AX = mybir.AxisListType

B, S, D = 16, 2048, 1024
H, DH = 16, 64
HF = 4096
NCORE = 8
NSEQ = 2
NT = S // 128
ND = D // 128
NHF = HF // 128
NHP = H // 2
TBLK = 1024
NBT = TBLK // 128
NBLK = S // TBLK
M32 = 1.5 * 2.0 ** 23
M64 = 1.5 * 2.0 ** 52
QB = 127.0
EPSQ = 1e-5
EPSLN = 1e-5


def make_consts():
    q = np.arange(128)[None, :]
    m = np.zeros((128, 640), np.float32)
    for j in range(5):
        krow = np.arange(128)[:, None]
        kd = 128 * j + krow - 256
        d = kd - q
        dil = (d != 0) & (d % 4 == 0) & (np.abs(d) <= 256)
        loc = (kd // 64) == (q // 64)
        m[:, 128 * j:128 * (j + 1)] = (dil | loc).astype(np.float32)
    mask01 = m.astype(ml_dtypes.bfloat16)
    ident = np.eye(128, dtype=np.float32)
    sel = np.zeros((16, 2048), np.float32)
    for t in range(16):
        sel[t, 128 * t:128 * (t + 1)] = 1.0
    return mask01, ident, sel


class Ctr:
    def __init__(self, h):
        self.h = h
        self.n = 0


def build_nc():
    nc = bass.Bass("TRN2", target_bir_lowering=False)
    x_h = nc.dram_tensor("x", [NSEQ * S, D], F32, kind="ExternalInput").ap()
    wq_h = nc.dram_tensor("wq", [D, D], F32, kind="ExternalInput").ap()
    wk_h = nc.dram_tensor("wk", [D, D], F32, kind="ExternalInput").ap()
    wv_h = nc.dram_tensor("wv", [D, D], F32, kind="ExternalInput").ap()
    wo_h = nc.dram_tensor("wo", [D, D], F32, kind="ExternalInput").ap()
    w1_h = nc.dram_tensor("w1", [HF, D], F32, kind="ExternalInput").ap()
    w2_h = nc.dram_tensor("w2", [D, HF], F32, kind="ExternalInput").ap()
    b1_h = nc.dram_tensor("b1c", [128, NHF], F32, kind="ExternalInput").ap()
    b2_h = nc.dram_tensor("b2r", [1, D], BF16, kind="ExternalInput").ap()
    mk_h = nc.dram_tensor("mask01", [128, 640], BF16, kind="ExternalInput").ap()
    id_h = nc.dram_tensor("ident", [128, 128], F32, kind="ExternalInput").ap()
    sel_h = nc.dram_tensor("sel16", [16, 2048], F32, kind="ExternalInput").ap()
    y_h = nc.dram_tensor("y", [NSEQ * S, D], F32, kind="ExternalOutput").ap()
    w1qt_h = nc.dram_tensor("w1qt", [NHF, ND, 128, 128], BF16, kind="Internal").ap()
    w2qt_h = nc.dram_tensor("w2qt", [NHF, ND, 128, 128], BF16, kind="Internal").ap()

    ops = {k: [] for k in ("sync", "scalar", "vector", "tensor", "gpsimd")}

    es = ExitStack()
    c_mask = es.enter_context(nc.sbuf_tensor("c_mask", [128, 640], BF16))
    c_id = es.enter_context(nc.sbuf_tensor("c_id", [128, 128], F32))
    c_ones = es.enter_context(nc.sbuf_tensor("c_ones", [128, 128], F32))
    c_onesb = es.enter_context(nc.sbuf_tensor("c_onesb", [1, 128], BF16))
    c_b1 = es.enter_context(nc.sbuf_tensor("c_b1", [128, NHF], F32))
    c_b2 = es.enter_context(nc.sbuf_tensor("c_b2", [1, D], BF16))
    s_det = es.enter_context(nc.sbuf_tensor("s_det", [128, 2], F32))
    s_M = es.enter_context(nc.sbuf_tensor("s_M", [128, 1], F32))
    s_eps = es.enter_context(nc.sbuf_tensor("s_eps", [128, 1], F32))
    s_wsc = es.enter_context(nc.sbuf_tensor("s_wsc", [128, 8], F32))
    s_st0 = es.enter_context(nc.sbuf_tensor("s_st0", [128, 64], F32))
    s_st1 = es.enter_context(nc.sbuf_tensor("s_st1", [128, 64], F32))
    s_wred = s_st1
    s_hst = es.enter_context(nc.sbuf_tensor("s_hst", [128, 24], F32))
    s_hrow = es.enter_context(nc.sbuf_tensor("s_hrow", [16, 128], F32))
    s_r = es.enter_context(nc.sbuf_tensor("s_r", [128, 16], F32))
    s_otm = es.enter_context(nc.sbuf_tensor("s_otm", [128, 1024], BF16))
    s_dump = es.enter_context(nc.sbuf_tensor("s_dump", [128, 1024], BF16))
    a_xnT = es.enter_context(nc.sbuf_tensor("a_xnT", [128, ND, S], BF16))
    a_big = es.enter_context(nc.sbuf_tensor("a_big", [128, 2 * ND * S], BF16))
    a_oT = es.enter_context(nc.sbuf_tensor("a_oT", [128, ND, S], BF16))
    a_vA = es.enter_context(nc.sbuf_tensor("a_vA", [128, 16640], BF16))
    a_aA = es.enter_context(nc.sbuf_tensor("a_aA", [128, 10240], BF16))
    a_aF = es.enter_context(nc.sbuf_tensor("a_aF", [128, 4096], F32))
    ps0 = es.enter_context(nc.psum_tensor("ps0", [128, 512], F32))
    ps1 = es.enter_context(nc.psum_tensor("ps1", [128, 512], F32))
    ps2 = es.enter_context(nc.psum_tensor("ps2", [128, 512], F32))
    ps3 = es.enter_context(nc.psum_tensor("ps3", [128, 512], F32))
    ps4 = es.enter_context(nc.psum_tensor("ps4", [128, 512], F32))
    ps5 = es.enter_context(nc.psum_tensor("ps5", [128, 512], F32))
    ps6 = es.enter_context(nc.psum_tensor("ps6", [128, 512], F32))
    ps7 = es.enter_context(nc.psum_tensor("ps7", [128, 512], F32))
    _XQ = es.enter_context(nc.semaphore("XQ"))
    _XR = es.enter_context(nc.semaphore("XR"))
    _WR = es.enter_context(nc.semaphore("WR"))
    _WQ = es.enter_context(nc.semaphore("WQ"))
    _VP = es.enter_context(nc.semaphore("VP"))
    _SP = es.enter_context(nc.semaphore("SP"))
    _TP = es.enter_context(nc.semaphore("TP"))
    _PQ = es.enter_context(nc.semaphore("PQ"))
    _QQ = es.enter_context(nc.semaphore("QQ"))
    _LQ = es.enter_context(nc.semaphore("LQ"))
    _GQ = es.enter_context(nc.semaphore("GQ"))
    _SQ = es.enter_context(nc.semaphore("SQ"))
    _HQ = es.enter_context(nc.semaphore("HQ"))
    _AQ = es.enter_context(nc.semaphore("AQ"))
    block = es.enter_context(nc.Block())
    with es:
        PS = [ps0, ps1, ps2, ps3, ps4, ps5, ps6, ps7]
        XQ, WQ, XR, WR = Ctr(_XQ), Ctr(_WQ), Ctr(_XR), Ctr(_WR)
        VP, SP, TP = Ctr(_VP), Ctr(_SP), Ctr(_TP)
        PQ, QQ, LQ, GQ = Ctr(_PQ), Ctr(_QQ), Ctr(_LQ), Ctr(_GQ)
        SQ, HQ, AQ = Ctr(_SQ), Ctr(_HQ), Ctr(_AQ)

        KT = a_big[:, 0:ND * S].rearrange("p (c t) -> p c t", c=ND)
        QT = a_big[:, ND * S:2 * ND * S].rearrange("p (c t) -> p c t", c=ND)
        HB = a_big[:].rearrange("p (c t) -> p c t", c=NHF)
        VTM = a_vA[:].rearrange("p (k c) -> p k c", k=16)
        WO = a_vA[:, 0:ND * 1024].rearrange("p (c n) -> p c n", c=ND)
        W1P = a_vA[:, 8192:8192 + 2 * ND * 128].rearrange("p (b d t) -> p b d t", b=2, d=ND)
        W2C = a_vA[:, 10240:10240 + 2 * 512].rearrange("p (b n) -> p b n", b=2)
        ATN = a_aA[:, 0:16 * 640].rearrange("p (h c) -> p h c", h=16)
        WVH = a_aA[:, 0:ND * 1024].rearrange("p (d n) -> p d n", d=ND)
        WCH = a_aA[:, 8192:8448].rearrange("p (b t) -> p b t", b=2)
        XQS = a_aA[:, 0:ND * 1024].rearrange("p (c t) -> p c t", c=ND)
        CBC = a_aA[:, 8192:9216]
        XQT = a_aF[:, 3072:4096]
        XT = a_aF[:, 0:1024]
        SELD = a_aF[0:8, 2048:3072]
        WQF = a_aF[:, 2048:3072]
        HALF = a_aF[:, 0:1024].rearrange("p (b n) -> p b n", b=2)
        OPX = a_aF[:, 0:1024]
        TMP2 = a_aF[:, 1024:2048]
        STT = [s_st0, s_st1]

        def emit(eng, fn, waits=(), inc=None, k=1):
            wl = tuple((c.h, v) for (c, v) in waits)
            if not wl:
                wl = ((_TP, 0),)
            ip = None
            if inc is not None:
                inc.n += k
                ip = (inc.h, k)

            def g(e, fn=fn, wl=wl, ip=ip):
                for (h, v) in wl:
                    e.wait_ge(h, v)
                instr = fn(e)
                if ip is not None:
                    instr.then_inc(*ip)
            ops[eng].append(g)
            return inc.n if inc is not None else None

        def fc(e):
            e.dma_start(out=c_mask[:], in_=mk_h).then_inc(XQ.h, 16)
            e.dma_start(out=c_id[:], in_=id_h).then_inc(XQ.h, 16)
            e.dma_start(out=c_b1[:], in_=b1_h).then_inc(XQ.h, 16)
            e.dma_start(out=c_b2[:], in_=b2_h).then_inc(XQ.h, 16)
        ops["sync"].append(fc)
        XQ.n += 64
        xq_consts = XQ.n

        emit("vector", lambda e: e.memset(s_det[:, 0:1], 1000.25), inc=VP)
        emit("vector", lambda e: e.tensor_scalar(s_det[:, 0:1], s_det[:, 0:1], M64, M64,
                                                 OP.add, OP.subtract), waits=[(VP, VP.n)], inc=VP)
        emit("vector", lambda e: e.tensor_scalar(s_det[:, 1:2], s_det[:, 0:1], 500.0, None, OP.is_ge),
             waits=[(VP, VP.n)], inc=VP)
        emit("vector", lambda e: e.tensor_scalar(s_det[:, 0:1], s_det[:, 0:1], 500.0, None, OP.is_lt),
             inc=VP)
        emit("vector", lambda e: e.tensor_scalar(s_det[:, 1:2], s_det[:, 1:2], M64, None, OP.mult),
             waits=[(VP, VP.n)], inc=VP)
        emit("vector", lambda e: e.tensor_scalar(s_det[:, 0:1], s_det[:, 0:1], M32, None, OP.mult),
             inc=VP)
        emit("vector", lambda e: e.tensor_add(s_M[:], s_det[:, 0:1], s_det[:, 1:2]),
             waits=[(VP, VP.n)], inc=VP)
        emit("vector", lambda e: e.memset(c_ones[:], 1.0), inc=VP)
        emit("vector", lambda e: e.memset(c_onesb[:], 1.0), inc=VP)
        emit("vector", lambda e: e.memset(s_eps[:], EPSLN), inc=VP)
        ones_cols = VTM[:, :, :].rearrange("p k (h c) -> p k h c", h=16)[:, :, :, 64:65]
        emit("vector", lambda e: e.memset(ones_cols, 1.0), inc=VP)
        vp_pre = VP.n

        wtiles = []
        for i in range(NHF):
            wtiles.append(("w1", w1_h[128 * i:128 * (i + 1), :], i))
        for r in range(8):
            for c in range(4):
                wtiles.append(("w2", w2_h[128 * r:128 * (r + 1), 1024 * c:1024 * (c + 1)], (r, c)))
        wsc_ready = [0]
        wscr_done = [0]
        wt_free = [0]
        _p1 = {"gate": 0, "k": 0, "gp": [0, 0]}
        _p2 = {"gate": 0, "k": 0, "tr": 0, "ev": 0, "trp": [0, 0], "wrp": [0, 0]}

        def emit_wp1():
            if _p1["k"] >= len(wtiles):
                return
            k = _p1["k"]
            _p1["k"] += 1
            kind, apw, _ = wtiles[k]
            par = k % 2
            wbuf = WQF if par == 0 else XQT
            WQc = WQ if par == 0 else WR
            wq_c = emit("sync", lambda e, apw=apw, wbuf=wbuf: e.dma_start(
                out=wbuf[:], in_=apw), waits=[(SP, _p1["gp"][par])], inc=WQc, k=16)
            vpn = emit("scalar", lambda e, k=k, wbuf=wbuf: e.activation(
                wbuf[:], wbuf[:], AFT.Abs, bias=0.0, scale=1.0,
                accum_out=s_wred[:, k:k + 1]), waits=[(WQc, wq_c)], inc=SP)
            _p1["gp"][par] = vpn
            _p1["gate"] = vpn

        def emit_wscale():
            emit("vector", lambda e: e.reduce_sum(s_wsc[:, 4:5], s_wred[:, 0:32], AX.X),
                 waits=[(SP, _p1["gate"])], inc=VP)
            emit("vector", lambda e: e.reduce_sum(s_wsc[:, 5:6], s_wred[:, 32:64], AX.X), inc=VP)

            def ft(e):
                e.matmul(PS[0][0:1, 0:1], c_ones[:, 0:1], s_wsc[:, 4:5], start=True, stop=True)
                return e.matmul(PS[0][0:1, 1:2], c_ones[:, 0:1], s_wsc[:, 5:6], start=True, stop=True)
            emit("tensor", ft, waits=[(VP, VP.n)], inc=TP)
            emit("vector", lambda e: e.tensor_scalar(s_wsc[0:1, 0:2], PS[0][0:1, 0:2],
                                                     1.0 / (HF * D), EPSQ, OP.mult, OP.max),
                 waits=[(TP, TP.n)], inc=VP)
            emit("tensor", lambda e: e.matmul(PS[0][:, 4:6], c_ones[0:1, :], s_wsc[0:1, 0:2],
                                              start=True, stop=True), waits=[(VP, VP.n)], inc=TP)
            emit("vector", lambda e: e.tensor_copy(s_wsc[:, 0:2], PS[0][:, 4:6]),
                 waits=[(TP, TP.n)], inc=VP)
            emit("vector", lambda e: e.reciprocal(s_wsc[:, 2:4], s_wsc[:, 0:2]),
                 waits=[(VP, VP.n)], inc=VP)
            emit("vector", lambda e: e.tensor_scalar(s_wsc[:, 6:7], s_wsc[:, 1:2], 1.0 / QB, None,
                                                     OP.mult), waits=[(VP, VP.n)], inc=VP)
            wsc_ready[0] = VP.n

        def emit_wp2():
            if _p2["k"] >= len(wtiles):
                return
            k = _p2["k"]
            _p2["k"] += 1
            kind, apw, meta = wtiles[k]
            rcol = 2 if kind == "w1" else 3
            par = k % 2
            wbuf = WQF if par == 0 else TMP2
            obuf = s_otm if par == 0 else s_dump
            wdw = [(VP, wsc_ready[0])]
            if _p2["trp"][par]:
                wdw.append((TP, _p2["trp"][par]))
            if par == 1 and k == 1:
                wdw += [(TP, a_tr_last[0]), (SP, xnT_ready[0]), (VP, stats_ready[0])]
            WQc = WQ if par == 0 else WR
            wq_c = emit("sync", lambda e, apw=apw, wbuf=wbuf: e.dma_start(
                out=wbuf[:], in_=apw), waits=wdw, inc=WQc, k=16)
            emit("vector", lambda e, rcol=rcol, wbuf=wbuf: e.tensor_scalar(
                wbuf[:], wbuf[:], s_wsc[:, rcol:rcol + 1], s_M[:],
                OP.mult, OP.add), waits=[(WQc, wq_c)], inc=VP)
            emit("vector", lambda e, wbuf=wbuf: e.tensor_scalar(
                wbuf[:], wbuf[:], s_M[:], 1.0, OP.subtract, OP.min),
                waits=[(VP, VP.n)], inc=VP)
            vq = emit("vector", lambda e, wbuf=wbuf: e.tensor_scalar(
                wbuf[:], wbuf[:], -1.0, None, OP.max), waits=[(VP, VP.n)], inc=VP)

            wtr = [(VP, vq)]
            if _p2["ev"]:
                wtr.append((SP, _p2["ev"]))

            def ftr(e, wbuf=wbuf):
                r = None
                for d2 in range(ND):
                    r = e.transpose(PS[d2 // 4][:, 128 * (d2 % 4):128 * (d2 % 4 + 1)],
                                    wbuf[:, 128 * d2:128 * (d2 + 1)], c_id[:])
                return r
            tpw = emit("tensor", ftr, waits=wtr, inc=TP)
            _p2["trp"][par] = tpw
            _p2["tr"] = tpw

            def fev(e, obuf=obuf):
                r = None
                for d2 in range(ND):
                    r = e.activation(obuf[:, 128 * d2:128 * (d2 + 1)],
                                     PS[d2 // 4][:, 128 * (d2 % 4):128 * (d2 % 4 + 1)],
                                     AFT.Copy, bias=0.0, scale=1.0)
                return r
            sqw = ((16 * (k - 1) + 31) // 32) * 32 if k >= 2 else 0
            spe = emit("scalar", fev, waits=[(TP, tpw), (SQ, sqw)],
                       inc=SP, k=ND)
            _p2["ev"] = spe

            if kind == "w1":
                i = meta

                def fw1(e, i=i, spc=spe, obuf=obuf):
                    e.wait_ge(SP.h, spc)
                    e.dma_start(out=w1qt_h[i, :, :, :].rearrange("d p t -> p d t"),
                                in_=obuf[:].rearrange("p (d t) -> p d t", d=ND)
                                ).then_inc(SQ.h, 16)
                ops["scalar"].append(fw1)
                SQ.n += 16
            else:
                r_, c_ = meta

                def fw2(e, r_=r_, c_=c_, spc=spe, obuf=obuf):
                    e.wait_ge(SP.h, spc)
                    e.dma_start(out=w2qt_h[8 * c_:8 * c_ + 8, r_, :, :]
                                .rearrange("a p t -> p a t"),
                                in_=obuf[:].rearrange("p (a t) -> p a t", a=ND)
                                ).then_inc(SQ.h, 16)
                ops["scalar"].append(fw2)
                SQ.n += 16
            _p2["wrp"][par] = SQ.n
            wscr_done[0] = SQ.n

        stats_ready = [0] * NSEQ
        xnT_ready = [0] * NSEQ
        srow_ready = [0] * NSEQ
        kqv_ready = [0] * NSEQ
        oT_ready = [0] * NSEQ
        oT_tp = [0] * NSEQ
        a_tr_last = [0] * NSEQ
        aF_free = [0]
        xnT_free = [0]
        d_prev = {"vp": 0, "sp": 0, "tp": 0, "wq": 0}
        oevac_gate = {}

        def emit_A(s, wp1_slots=0):
            st = STT[s]
            tile_done = {}
            mrs_done = {}
            xtr_done = {}
            xdma_free = {}
            for i in range(NT):
                XTb = XT if i % 2 == 0 else TMP2
                row0 = 128 * (NT * s + i)
                base = 48 if i % 2 == 0 else 56
                csum, csq, cvar, crst, cmrs, cxs, ccc = (base + j for j in range(7))
                w = [(XQ, xq_consts)]
                if i >= 2:
                    w += [(VP, xdma_free[i - 2]), (TP, xtr_done[i - 2])]
                if s > 0:
                    w.append((AQ, aF_free[0]))
                XQc = XQ if i % 2 == 0 else XR
                xq_i = emit("sync", lambda e, row0=row0, XTb=XTb: e.dma_start(
                    out=XTb[:], in_=x_h[row0:row0 + 128, :]), waits=w, inc=XQc, k=16)
                wv = [(XQc, xq_i)]
                if i >= 2:
                    wv.append((SP, mrs_done[i - 2]))
                vp_sum = emit("vector", lambda e, csum=csum, XTb=XTb: e.reduce_sum(
                    st[:, csum:csum + 1], XTb[:], AX.X), waits=wv, inc=VP)
                ws = [(XQc, xq_i)]
                if i >= 2:
                    ws.append((VP, tile_done[i - 2]))
                emit("scalar", lambda e, csq=csq, XTb=XTb: e.activation(
                    s_dump[:, :], XTb[:], AFT.Square, bias=0.0, scale=1.0,
                    accum_out=st[:, csq:csq + 1]), waits=ws, inc=SP)
                emit("scalar", lambda e, csum=csum, cvar=cvar: e.activation(
                    st[:, cvar:cvar + 1], st[:, csum:csum + 1], AFT.Square, bias=0.0, scale=1.0 / D),
                    waits=[(VP, vp_sum)], inc=SP)
                sp_a = emit("scalar", lambda e, csum=csum, cmrs=cmrs: e.activation(
                    st[:, cmrs:cmrs + 1], st[:, csum:csum + 1], AFT.Copy, bias=0.0, scale=-1.0 / D),
                    inc=SP)
                vp_var = emit("vector", lambda e, csq=csq, cvar=cvar: e.scalar_tensor_tensor(
                    st[:, cvar:cvar + 1], st[:, csq:csq + 1], 1.0 / D, st[:, cvar:cvar + 1],
                    OP.mult, OP.subtract), waits=[(SP, sp_a)], inc=VP)
                sp_std = emit("scalar", lambda e, cvar=cvar, crst=crst: e.activation(
                    st[:, crst:crst + 1], st[:, cvar:cvar + 1], AFT.Sqrt, bias=s_eps[:], scale=1.0),
                    waits=[(VP, vp_var)], inc=SP)
                vp_rst = emit("vector", lambda e, crst=crst: e.reciprocal(
                    st[:, crst:crst + 1], st[:, crst:crst + 1]), waits=[(SP, sp_std)], inc=VP)
                sp_mrs = emit("scalar", lambda e, cmrs=cmrs, crst=crst: e.activation(
                    st[:, cmrs:cmrs + 1], st[:, cmrs:cmrs + 1], AFT.Copy, bias=0.0,
                    scale=st[:, crst:crst + 1]), waits=[(VP, vp_rst)], inc=SP)
                mrs_done[i] = sp_mrs
                vp_xn = emit("vector", lambda e, crst=crst, cmrs=cmrs, XTb=XTb: e.tensor_scalar(
                    XTb[:], XTb[:], st[:, crst:crst + 1], st[:, cmrs:cmrs + 1],
                    OP.mult, OP.add), waits=[(SP, sp_mrs)], inc=VP)
                vp_xs = emit("vector", lambda e, cxs=cxs, XTb=XTb: e.reduce_max(
                    st[:, cxs:cxs + 1], XTb[:], AX.X, apply_absolute_value=True),
                    waits=[(VP, vp_xn)], inc=VP)
                xdma_free[i] = vp_xs
                emit("vector", lambda e, cxs=cxs, i=i: e.tensor_scalar(
                    st[:, 16 + i:16 + i + 1], st[:, cxs:cxs + 1], EPSQ, 1.0 / QB, OP.max, OP.mult),
                    waits=[(VP, vp_xs)], inc=VP)
                vp_c = emit("vector", lambda e, i=i, ccc=ccc: e.reciprocal(
                    st[:, ccc:ccc + 1], st[:, 16 + i:16 + i + 1]), waits=[(VP, VP.n)], inc=VP)
                vp_rc = emit("vector", lambda e, i=i, crst=crst, ccc=ccc: e.tensor_scalar(
                    st[:, i:i + 1], st[:, crst:crst + 1], st[:, ccc:ccc + 1], None, OP.mult),
                    waits=[(VP, vp_c)], inc=VP)
                vp_mc = emit("vector", lambda e, i=i, cmrs=cmrs, ccc=ccc: e.tensor_scalar(
                    st[:, 32 + i:32 + i + 1], st[:, cmrs:cmrs + 1], st[:, ccc:ccc + 1], None,
                    OP.mult), waits=[(VP, vp_rc), (SP, sp_mrs)], inc=VP)
                tile_done[i] = vp_mc

                def ftr(e, XTb=XTb):
                    r = None
                    for d2 in range(ND):
                        r = e.transpose(PS[4 + d2 // 4][:, 128 * (d2 % 4):128 * (d2 % 4 + 1)],
                                        XTb[:, 128 * d2:128 * (d2 + 1)], c_id[:])
                    return r
                tp_x = emit("tensor", ftr, waits=[(VP, vp_xn)], inc=TP)
                xtr_done[i] = tp_x

                def fxe(e, i=i):
                    r = None
                    for d2 in range(ND):
                        r = e.activation(a_xnT[:, d2, 128 * i:128 * (i + 1)],
                                         PS[4 + d2 // 4][:, 128 * (d2 % 4):128 * (d2 % 4 + 1)],
                                         AFT.Copy, bias=0.0, scale=1.0)
                    return r
                xnev = emit("scalar", fxe, waits=[(TP, tp_x)], inc=SP, k=ND)
                for _ in range(wp1_slots):
                    emit_wp1()
            stats_ready[s] = VP.n
            xnT_ready[s] = xnev
            srow_ready[s] = VP.n
            a_tr_last[s] = xtr_done[NT - 1]

        def emit_B(s, wp1=False):
            gq_base = GQ.n
            chunk_done = {}
            evac_done = {}
            for pi, (wh, dstT) in enumerate(((wq_h, QT), (wk_h, KT))):
                for h8 in range(ND):
                    for d in range(ND):
                        ci = pi * 64 + h8 * ND + d
                        w = []
                        if ci >= 2:
                            w.append((TP, chunk_done[ci - 2]))
                        if s > 0 and ci == 0:
                            w.append((TP, d_prev["tp"]))
                        emit("gpsimd", lambda e, wh=wh, h8=h8, d=d, ci=ci: e.dma_start(
                            out=WCH[:, ci % 2, :],
                            in_=wh[128 * d:128 * (d + 1), 128 * h8:128 * (h8 + 1)]),
                            waits=w, inc=GQ, k=16)
                        wq_c = min(gq_base + 16 * (ci - ci % 2 + 2), gq_base + 16 * 128)
                        wt = [(GQ, wq_c)]
                        if d == 0:
                            wt.append((SP, xnT_ready[s]))
                            key = (pi, h8 - 1) if h8 > 0 else ((pi - 1, ND - 1) if pi > 0 else None)
                            if key is not None:
                                wt.append((VP, evac_done[key]))
                            if s > 0:
                                wt.append((VP, d_prev["vp"]))
                                wt.append((SP, d_prev["sp"]))

                        def fmm(e, ci=ci, d=d):
                            r = None
                            for st4 in range(4):
                                r = e.matmul(PS[4 + st4][:, :], WCH[:, ci % 2, :],
                                             a_xnT[:, d, 512 * st4:512 * (st4 + 1)],
                                             start=(d == 0), stop=(d == ND - 1))
                            return r
                        tpc = emit("tensor", fmm, waits=wt, inc=TP)
                        chunk_done[ci] = tpc
                        if wp1 and ci >= 32:
                            emit_wp2()

                    def fev(e, h8=h8, dstT=dstT):
                        r = None
                        for st4 in range(4):
                            r = e.tensor_copy(dstT[:, h8, 512 * st4:512 * (st4 + 1)], PS[4 + st4][:, :])
                        return r
                    wv = [(TP, TP.n)]
                    if s > 0:
                        wv.append((TP, d_prev["tp"]))
                    ev = emit("vector", fev, waits=wv, inc=VP, k=4)
                    evac_done[(pi, h8)] = ev
                    if wp1:
                        emit_wp1()
            w0 = [(XQ, xq_consts), (TP, chunk_done[127])]
            if wp1 and _p2["ev"]:
                w0.append((SP, _p2["ev"]))
            if s > 0:
                w0.append((VP, d_prev["vp"]))
                w0.append((TP, d_prev["tp"]))

            def fwv(e, wl=tuple((c.h, v) for (c, v) in w0 if v > 0)):
                for (hh, v) in wl:
                    e.wait_ge(hh, v)
                for d in range(ND):
                    e.dma_start(out=WVH[:, d, :], in_=wv_h[128 * d:128 * (d + 1), :]).then_inc(GQ.h, 16)
            ops["gpsimd"].append(fwv)
            GQ.n += ND * 16
            wq_v = GQ.n
            wones = [(VP, VP.n)]
            if s > 0:
                wones += [(VP, d_prev["vp"]), (TP, d_prev["tp"]), (SP, d_prev["sp"])]
            vp_ones = emit("vector", lambda e: e.memset(ones_cols, 1.0), waits=wones, inc=VP)
            v_evac = {}
            for kt in range(16):
                wt = [(GQ, wq_v), (VP, evac_done[(1, ND - 1)]), (VP, srow_ready[s])]
                if kt >= 2:
                    wt.append((VP, v_evac[kt - 2]))
                if s > 0 and kt == 0:
                    wt.append((VP, d_prev["vp"]))
                    wt.append((SP, d_prev["sp"]))

                def fmm(e, kt=kt):
                    r = None
                    for d in range(ND):
                        for half in range(2):
                            r = e.matmul(PS[(kt % 2) * 2 + half][:, :],
                                         a_xnT[:, d, 128 * kt:128 * (kt + 1)],
                                         WVH[:, d, 512 * half:512 * (half + 1)],
                                         start=(d == 0), stop=(d == ND - 1))
                    return r
                tpv = emit("tensor", fmm, waits=wt, inc=TP)

                def fev(e, kt=kt):
                    dst = VTM[:, kt, :].rearrange("p (h c) -> p h c", h=16)[:, :, 0:64]
                    e.tensor_copy(dst[:, 0:8, :],
                                  PS[(kt % 2) * 2][:, :].rearrange("p (h c) -> p h c", h=8))
                    return e.tensor_copy(dst[:, 8:16, :],
                                         PS[(kt % 2) * 2 + 1][:, :].rearrange("p (h c) -> p h c", h=8))
                ev = emit("vector", fev, waits=[(TP, tpv)], inc=VP, k=2)
                v_evac[kt] = ev
                if wp1:
                    emit_wp1()
            kqv_ready[s] = VP.n

        def emit_C(s, wp2_per_tile=0):
            vp_c_entry = VP.n
            otr_done = {}
            exp_done = {}
            mask_done = {}
            oevac_done = {}
            oevac_sp = {}
            atn_gate = {}
            for t in range(NT):
                lo = max(0, 128 * t - 256)
                hi = min(S, 128 * t + 384)
                nkt = (hi - lo) // 128
                kt0 = lo // 128
                j0 = kt0 - (t - 2)
                for hp in range(NHP):
                    par = hp % 2
                    for h01 in range(2):
                        h = 2 * hp + h01
                        pc = PS[(0 if par == 0 else 4) if h01 == 0 else (2 if par == 0 else 5)]
                        pc2 = PS[1 if h01 == 0 else 3][:, 128 * par:128 * (par + 1)]
                        rows = slice(64 * h01, 64 * h01 + 64)
                        w = []
                        if t == 0 and hp == 0 and h01 == 0:
                            w.append((VP, max(kqv_ready[s], vp_c_entry)))
                        if t == 0 and hp < 2 and _p2["ev"]:
                            w.append((SP, _p2["ev"]))
                        if hp >= 2:
                            w.append((SP, exp_done[(t, hp - 2, h01)]))
                        elif t > 0:
                            w.append((VP, oevac_done[t - 1]))
                            w.append((SP, oevac_sp[t - 1]))

                        def fsc(e, t=t, hp=hp, rows=rows, nkt=nkt, kt0=kt0, pc=pc, pc2=pc2):
                            r = None
                            for j in range(nkt):
                                dst = pc[:, 128 * j:128 * (j + 1)] if j < 4 else pc2[:, 0:128]
                                r = e.matmul(dst, KT[rows, hp, 128 * (kt0 + j):128 * (kt0 + j + 1)],
                                             QT[rows, hp, 128 * t:128 * (t + 1)],
                                             start=True, stop=True)
                            return r
                        tpsc = emit("tensor", fsc, waits=w, inc=TP)
                        n1 = min(nkt, 4) * 128
                        wx = [(TP, tpsc)]
                        ag = atn_gate.get((t, h), 0)
                        if ag:
                            wx.append((TP, ag))

                        def fex(e, h=h, pc=pc, pc2=pc2, nkt=nkt, n1=n1):
                            r = e.activation(ATN[:, h, 0:n1], pc[:, 0:n1], AFT.Exp,
                                             bias=0.0, scale=0.125)
                            if nkt > 4:
                                r = e.activation(ATN[:, h, 512:640], pc2[:, 0:128], AFT.Exp,
                                                 bias=0.0, scale=0.125)
                            return r
                        spx = emit("scalar", fex, waits=wx, inc=SP)
                        exp_done[(t, hp, h01)] = spx
                        vpm = emit("vector", lambda e, h=h, nkt=nkt, j0=j0: e.tensor_tensor(
                            ATN[:, h, 0:128 * nkt], ATN[:, h, 0:128 * nkt],
                            c_mask[:, 128 * j0:128 * (j0 + nkt)], OP.mult),
                            waits=[(SP, spx)], inc=VP)
                        mask_done[(t, h)] = vpm
                wav = [(VP, max(mask_done[(t, h)] for h in range(16)))]
                if t > 0:
                    wav.append((VP, oevac_done[t - 1]))
                    wav.append((SP, oevac_sp[t - 1]))

                def fav(e, nkt=nkt, kt0=kt0):
                    r = None
                    for hp in range(NHP):
                        for h01 in range(2):
                            h = 2 * hp + h01
                            c0 = 256 * (hp % 2) + 65 * h01
                            dst = PS[hp // 2][:, c0:c0 + 65]
                            for j in range(nkt):
                                r = e.matmul(dst, ATN[:, h, 128 * j:128 * (j + 1)],
                                             VTM[:, kt0 + j, 65 * h:65 * h + 65],
                                             start=(j == 0), stop=(j == nkt - 1))
                    return r
                tpav = emit("tensor", fav, waits=wav, inc=TP)
                oevac_gate[t] = tpav
                for h in range(16):
                    atn_gate[(t + 1, h)] = tpav

                def fr(e):
                    r = None
                    for qq in range(4):
                        v2 = PS[qq][:, :].rearrange("p (a c) -> p a c", a=2)
                        r = e.reciprocal(s_r[:, 4 * qq + 0:4 * qq + 2],
                                         v2[:, :, 64:65].rearrange("p a c -> p (a c)"))
                        r = e.reciprocal(s_r[:, 4 * qq + 2:4 * qq + 4],
                                         v2[:, :, 129:130].rearrange("p a c -> p (a c)"))
                    return r
                vpr = emit("vector", fr, waits=[(TP, tpav)], inc=VP, k=8)

                def foe(e):
                    r = None
                    for hp in range(NHP):
                        for h01 in range(2):
                            h = 2 * hp + h01
                            c0 = 256 * (hp % 2) + 65 * h01
                            rcol = 4 * (hp // 2) + (hp % 2) + 2 * h01
                            r = e.tensor_scalar(XQT[:, 64 * h:64 * (h + 1)],
                                                PS[hp // 2][:, c0:c0 + 64],
                                                s_r[:, rcol:rcol + 1], None, OP.mult)
                    return r
                woe = [(VP, vpr)]
                if t >= 1:
                    woe.append((TP, otr_done[t - 1]))
                vpoe = emit("vector", foe, waits=woe, inc=VP, k=16)
                oevac_done[t] = vpoe

                def ftr(e):
                    r = None
                    for c in range(ND):
                        r = e.transpose(PS[c // 4][:, 128 * (c % 4):128 * (c % 4 + 1)],
                                        XQT[:, 128 * c:128 * (c + 1)], c_id[:])
                    return r
                wtro = [(VP, vpoe)]
                if t >= 1:
                    wtro.append((SP, oevac_sp[t - 1]))
                tp_o = emit("tensor", ftr, waits=wtro, inc=TP)
                otr_done[t] = tp_o

                def foev(e, t=t):
                    r = None
                    for c in range(ND):
                        r = e.activation(a_oT[:, c, 128 * t:128 * (t + 1)],
                                         PS[c // 4][:, 128 * (c % 4):128 * (c % 4 + 1)],
                                         AFT.Copy, bias=0.0, scale=1.0)
                    return r
                oevac_sp[t] = emit("scalar", foev, waits=[(TP, tp_o)], inc=SP, k=ND)
            oT_ready[s] = oevac_sp[NT - 1]
            oT_tp[s] = otr_done[NT - 1]

        def emit_D(s):
            st_ = STT[s]
            def fwo(e, need=oevac_gate[NT - 1]):
                e.wait_ge(TP.h, need)
                for c in range(ND):
                    e.dma_start(out=WO[:, c, :], in_=wo_h[128 * c:128 * (c + 1), :]).then_inc(GQ.h, 16)
            ops["gpsimd"].append(fwo)
            GQ.n += ND * 16
            wq_wo = GQ.n
            gelu_gate = [0]
            opj_copy = {}
            halfev_vp = {}
            half_written = {}
            for b in range(NBLK):
                if b == 0:
                    def gsel(e, need=max(sel_last_tp[0], oevac_gate[NT - 1], wp2_tp[0])):
                        e.wait_ge(TP.h, need)
                        e.dma_start(out=SELD, in_=sel_h[0:8, 0:1024]).then_inc(LQ.h, 16)
                    ops["sync"].append(gsel)
                    LQ.n += 16
                xq_p3 = {}
                xqtr = {}
                sp_ev = {}
                for j in range(NBT):
                    t = NBT * b + j
                    row0 = 128 * (NT * s + t)
                    wxd = [(VP, xq_p3.get(j - 1, d_xq_gate[0])), (TP, d_pmaxtr[0]),
                           (XQ, XQ.n)]
                    if j >= 1:
                        wxd.append((TP, xqtr[j - 1]))
                    elif b == 0:
                        wxd.append((TP, oT_tp[s]))
                    else:
                        wxd.append((TP, xqtr_prev[0]))
                    xd = emit("sync", lambda e, row0=row0: e.dma_start(
                        out=XQT[:, :], in_=x_h[row0:row0 + 128, :]), waits=wxd, inc=XQ, k=16)
                    p1 = emit("vector", lambda e, t=t: e.tensor_scalar(
                        XQT[:, :], XQT[:, :], st_[:, t:t + 1], st_[:, 32 + t:33 + t],
                        OP.mult, OP.add), waits=[(XQ, xd), (VP, stats_ready[s])], inc=VP)
                    p2 = emit("vector", lambda e: e.tensor_scalar(
                        XQT[:, :], XQT[:, :], s_M[:], s_M[:], OP.add, OP.subtract),
                        waits=[(VP, p1)], inc=VP)
                    p3 = emit("vector", lambda e, t=t: e.tensor_scalar(
                        XQT[:, :], XQT[:, :], st_[:, 16 + t:17 + t], None, OP.mult),
                        waits=[(VP, p2)], inc=VP)
                    xq_p3[j] = p3

                    def ftq(e):
                        r = None
                        for c2 in range(ND):
                            r = e.transpose(PS[6 + c2 // 4][:, 128 * (c2 % 4):128 * (c2 % 4 + 1)],
                                            XQT[:, 128 * c2:128 * (c2 + 1)], c_id[:])
                        return r
                    wtq_ = [(VP, p3)]
                    if j >= 1:
                        wtq_.append((SP, sp_ev[j - 1]))
                    else:
                        wtq_.append((SP, max(wp2_sp[0], xq_sp_prev[0])))
                    tp_q = emit("tensor", ftq, waits=wtq_, inc=TP)
                    xqtr[j] = tp_q

                    def fqe(e, j=j):
                        r = None
                        for c2 in range(ND):
                            r = e.activation(XQS[:, c2, 128 * j:128 * (j + 1)],
                                             PS[6 + c2 // 4][:, 128 * (c2 % 4):128 * (c2 % 4 + 1)],
                                             AFT.Copy, bias=0.0, scale=1.0)
                        return r
                    xq_done_sp = emit("scalar", fqe, waits=[(TP, tp_q)], inc=SP, k=ND)
                    sp_ev[j] = xq_done_sp
                xqtr_prev[0] = xqtr[NBT - 1]
                xq_sp_prev[0] = xq_done_sp
                xq_done_tp = xqtr[NBT - 1]
                d_xq_gate[0] = xq_p3[NBT - 1]
                pq_base = PQ.n
                panel_done = {}
                sp_gelu = {}
                for hc in range(NHF):
                    w = [(VP, wsc_ready[0]), (SQ, wscr_done[0]), (TP, w1slot[hc % 2])]
                    if hc < 2:
                        w.append((TP, oevac_gate[NT - 1]))
                    emit("sync", lambda e, hc=hc: e.dma_start(
                        out=W1P[:, hc % 2, :, :],
                        in_=w1qt_h[hc, :, :, :].rearrange("d p t -> p d t")), waits=w, inc=PQ, k=16)
                    xq_p = min(pq_base + 16 * (hc - hc % 2 + 2), pq_base + 16 * NHF)
                    wt = [(PQ, xq_p)]
                    if hc == 0:
                        wt += [(SP, xq_done_sp), (TP, gelu_gate[0])]
                        if b > 0:
                            wt += [(SP, opj_sp_last[0])]
                    if hc >= 2:
                        wt.append((SP, sp_gelu[hc - 2]))

                    def fm1(e, hc=hc):
                        r = None
                        for half in range(2):
                            for d in range(ND):
                                r = e.matmul(PS[2 * (hc % 2) + half][:, :], W1P[:, hc % 2, d, :],
                                             XQS[:, d, 512 * half:512 * (half + 1)],
                                             start=(d == 0), stop=(d == ND - 1))
                        return r
                    tp1 = emit("tensor", fm1, waits=wt, inc=TP)
                    panel_done[hc] = tp1
                    w1slot[hc % 2] = tp1
                    d_ffn1_last[0] = tp1

                    def fg(e, hc=hc):
                        r = None
                        for half in range(2):
                            r = e.activation(HB[:, hc, 512 * half:512 * (half + 1)],
                                             PS[2 * (hc % 2) + half][:, :], AFT.Gelu,
                                             bias=c_b1[:, hc:hc + 1], scale=s_wsc[:, 0:1])
                        return r
                    spg = emit("scalar", fg, waits=[(TP, tp1)], inc=SP)
                    sp_gelu[hc] = spg
                    if hc == 1:
                        emit("vector", lambda e: e.tensor_tensor(
                            s_dump[:, :], HB[:, 0, :], HB[:, 1, :], OP.max),
                            waits=[(SP, spg)], inc=VP)
                        emit("vector", lambda e: e.tensor_tensor(
                            s_otm[:, :], HB[:, 0, :], HB[:, 1, :], OP.min),
                            waits=[(VP, VP.n)], inc=VP)
                    elif hc >= 2:
                        emit("vector", lambda e, hc=hc: e.tensor_tensor(
                            s_dump[:, :], s_dump[:, :], HB[:, hc, :], OP.max),
                            waits=[(SP, spg), (VP, VP.n)], inc=VP)
                        emit("vector", lambda e, hc=hc: e.tensor_tensor(
                            s_otm[:, :], s_otm[:, :], HB[:, hc, :], OP.min),
                            waits=[(VP, VP.n)], inc=VP)
                emit("vector", lambda e: e.tensor_scalar(
                    s_otm[:, :], s_otm[:, :], -1.0, None, OP.mult), waits=[(VP, VP.n)], inc=VP)
                emit("vector", lambda e: e.tensor_tensor(
                    XQT[:, :], s_dump[:, :], s_otm[:, :], OP.max), waits=[(VP, VP.n)], inc=VP)
                vp_tree = VP.n
                for j in range(NBT):
                    tpt = emit("tensor", lambda e, j=j: e.transpose(
                        PS[4 + (j % 2)][:, 0:128],
                        XQT[:, 128 * j:128 * (j + 1)], c_id[:]),
                        waits=[(VP, vp_tree)] + ([(VP, hst_read[0])] if j >= 2 else []), inc=TP)
                    d_pmaxtr[0] = tpt
                    vpr1 = emit("vector", lambda e, j=j: e.reduce_max(
                        s_hst[:, j:j + 1], PS[4 + (j % 2)][:, 0:128], AX.X,
                        apply_absolute_value=True), waits=[(TP, tpt)], inc=VP)
                    hst_read[0] = vpr1
                    emit("vector", lambda e, j=j: e.tensor_scalar(
                        s_hst[:, j:j + 1], s_hst[:, j:j + 1], EPSQ, None, OP.max),
                        waits=[(VP, VP.n)], inc=VP)
                    emit("vector", lambda e, j=j: e.tensor_scalar(
                        s_hst[:, 8 + j:9 + j], s_hst[:, j:j + 1], s_wsc[:, 6:7], None, OP.mult),
                        waits=[(VP, VP.n)], inc=VP)
                    emit("vector", lambda e, j=j: e.reciprocal(
                        s_hst[:, 16 + j:17 + j], s_hst[:, j:j + 1]), waits=[(VP, VP.n - 1)], inc=VP)
                    emit("vector", lambda e, j=j: e.tensor_scalar(
                        s_hst[:, 16 + j:17 + j], s_hst[:, 16 + j:17 + j], QB, None, OP.mult),
                        waits=[(VP, VP.n)], inc=VP)
                vp_hst = VP.n
                tph = emit("tensor", lambda e: e.transpose(PS[4][0:8, 0:128], s_hst[:, 16:24],
                                                           c_id[:]),
                           waits=[(VP, vp_hst)], inc=TP)
                vph = emit("vector", lambda e: e.tensor_copy(s_hrow[0:8, 0:128], PS[4][0:8, 0:128]),
                           waits=[(TP, tph)], inc=VP)

                if b > 0:
                    def gsel0(e, need=TP.n):
                        e.wait_ge(TP.h, need)
                        e.dma_start(out=SELD, in_=sel_h[0:8, 0:1024]).then_inc(LQ.h, 16)
                    ops["sync"].append(gsel0)
                    LQ.n += 16
                xq_sel0 = LQ.n

                def fmhb(e, vph=vph, xq_sel0=xq_sel0, hr=hst_read[0]):
                    e.wait_ge(LQ.h, xq_sel0)
                    e.wait_ge(VP.h, vph)
                    e.wait_ge(VP.h, hr)
                    r = None
                    for j in range(NBT):
                        r = e.matmul(PS[4 + j // 4][:, 128 * (j % 4):128 * (j % 4 + 1)],
                                     SELD[:, 128 * j:128 * (j + 1)], s_hrow[0:8, 0:128],
                                     start=True, stop=True)
                    return r
                tpm = emit("tensor", fmhb, waits=[], inc=TP)
                sel_last_tp[0] = tpm
                vphb = emit("vector", lambda e: e.tensor_copy(CBC.bitcast(BF16)[:, 0:512], PS[4][:, :]),
                            waits=[(TP, TP.n)], inc=VP)
                vphb = emit("vector", lambda e: e.tensor_copy(CBC.bitcast(BF16)[:, 512:1024],
                                                              PS[5][:, :]), inc=VP)
                d_cbc_free[0] = 0
                for hc in range(NHF):
                    emit("vector", lambda e, hc=hc: e.tensor_tensor(
                        HB[:, hc, :], HB[:, hc, :], CBC.bitcast(BF16)[:, :], OP.mult),
                        waits=[(VP, vphb)], inc=VP)
                    emit("vector", lambda e, hc=hc: e.tensor_scalar(
                        HB[:, hc, :], HB[:, hc, :], s_M[:], s_M[:], OP.add, OP.subtract),
                        waits=[(VP, VP.n)], inc=VP)
                vp_hq = VP.n
                d_cbc_free[0] = vp_hq
                for dmh in range(2):
                    qq_base = QQ.n
                    ch_done = {}
                    for hc in range(NHF):
                        w = [(SQ, wscr_done[0]), (TP, w2slot[hc % 2])]
                        if hc < 2:
                            w.append((TP, oevac_gate[NT - 1]))
                        emit("sync", lambda e, hc=hc, dmh=dmh: e.dma_start(
                            out=W2C[:, hc % 2, :].rearrange("p (a t) -> p a t", a=4),
                            in_=w2qt_h[hc, 4 * dmh:4 * dmh + 4, :, :].rearrange("a p t -> p a t")),
                            waits=w, inc=QQ, k=16)
                        xq2 = min(qq_base + 16 * (hc - hc % 2 + 2), qq_base + 16 * NHF)
                        wt = [(QQ, xq2)]
                        if hc == 0:
                            wt.append((VP, vp_hq))
                            if dmh == 1:
                                wt.append((VP, halfev_vp[(b, 0, NBT - 1)]))

                        def fm2(e, hc=hc):
                            r = None
                            for j in range(NBT):
                                r = e.matmul(PS[j][:, :], HB[:, hc, 128 * j:128 * (j + 1)],
                                             W2C[:, hc % 2, :], start=(hc == 0), stop=(hc == NHF - 1))
                            return r
                        tp2 = emit("tensor", fm2, waits=wt, inc=TP)
                        ch_done[hc] = tp2
                        w2slot[hc % 2] = tp2
                    tp_f2 = TP.n
                    for j in range(NBT):
                        t = NBT * b + j
                        row0 = 128 * (NT * s + t)
                        w = [(TP, tp_f2), (HQ, halfslot[j % 2]), (VP, d_opx[0])]
                        hv = emit("vector", lambda e, j=j: e.tensor_scalar(
                            HALF[:, j % 2, :], PS[j][:, :], s_hst[:, 8 + j:9 + j], None, OP.mult),
                            waits=w, inc=VP)
                        halfev_vp[(b, dmh, j)] = hv
                        hw = emit("gpsimd", lambda e, j=j, row0=row0, dmh=dmh: e.dma_start(
                            out=y_h[row0:row0 + 128, 512 * dmh:512 * (dmh + 1)],
                            in_=HALF[:, j % 2, :]), waits=[(VP, hv), (HQ, HQ.n)], inc=HQ, k=16)
                        half_written[(dmh, j)] = hw
                        halfslot[j % 2] = hw
                gelu_gate[0] = TP.n
                for j in range(NBT):
                    t = NBT * b + j
                    row0 = 128 * (NT * s + t)
                    pa, pb = PS[2 * (j % 2)], PS[2 * (j % 2) + 1]
                    w = [(GQ, wq_wo), (SP, oT_ready[s]),
                         (VP, halfev_vp[(b, 1, 2 * (j % 2))]),
                         (VP, halfev_vp[(b, 1, 2 * (j % 2) + 1)]), (XQ, xq_consts)]
                    if j >= 2:
                        w.append((SP, opj_copy[j - 2]))

                    def fop(e, t=t, pa=pa, pb=pb):
                        e.matmul(pa[:, :], c_onesb[0:1, :], c_b2[0:1, 0:512], start=True, stop=False)
                        e.matmul(pb[:, :], c_onesb[0:1, :], c_b2[0:1, 512:1024], start=True, stop=False)
                        r = None
                        for c in range(ND):
                            e.matmul(pa[:, :], a_oT[:, c, 128 * t:128 * (t + 1)],
                                     WO[:, c, 0:512], start=False, stop=(c == ND - 1))
                            r = e.matmul(pb[:, :], a_oT[:, c, 128 * t:128 * (t + 1)],
                                         WO[:, c, 512:1024], start=False, stop=(c == ND - 1))
                        return r
                    tpo = emit("tensor", fop, waits=w, inc=TP)

                    def fcp(e, j=j, pa=pa, pb=pb):
                        e.copy(TMP2[:, 0:512], pa[:, :])
                        return e.copy(TMP2[:, 512:1024], pb[:, :])
                    spc = emit("scalar", fcp, waits=[(TP, tpo), (AQ, tmp2slot[0])],
                               inc=SP, k=2)
                    opj_copy[j] = spc
                    wyr = [(SP, spc), (HQ, HQ.n),
                           (VP, max(d_opx[0], halfev_vp[(b, 1, NBT - 1)])), (XQ, XQ.n)]
                    yr = emit("sync", lambda e, row0=row0: e.dma_start(
                        out=OPX, in_=y_h[row0:row0 + 128, :]), waits=wyr, inc=XQ, k=16)
                    a1 = emit("vector", lambda e: e.tensor_tensor(
                        TMP2[:, :], TMP2[:, :], OPX, OP.add),
                        waits=[(XQ, yr), (SP, spc)], inc=VP)
                    xr = emit("sync", lambda e, row0=row0: e.dma_start(
                        out=OPX, in_=x_h[row0:row0 + 128, :]), waits=[(VP, a1)], inc=XQ, k=16)
                    a2 = emit("vector", lambda e: e.tensor_tensor(
                        TMP2[:, :], TMP2[:, :], OPX, OP.add),
                        waits=[(XQ, xr), (VP, a1)], inc=VP)
                    d_opx[0] = a2
                    fin = emit("gpsimd", lambda e, row0=row0: e.dma_start(
                        out=y_h[row0:row0 + 128, :], in_=TMP2[:, :]),
                        waits=[(VP, a2), (AQ, AQ.n)], inc=AQ, k=16)
                    tmp2slot[0] = fin
                opj_sp_last[0] = opj_copy[NBT - 1]
            d_prev["vp"], d_prev["sp"], d_prev["tp"], d_prev["wq"] = VP.n, SP.n, TP.n, AQ.n
            aF_free[0] = AQ.n

        selbuf_gate = {}
        d_xq_gate = [0]
        d_pmaxtr = [0]
        d_opx = [0]
        xqtr_prev = [0]
        xq_sp_prev = [0]
        d_ffn1_last = [0]
        w1slot = [0, 0]
        w2slot = [0, 0]
        halfslot = [0, 0]
        tmp2slot = [0]
        sel_last_tp = [0]
        wp2_tp = [0]
        wp2_sp = [0]
        spb_free = [0]
        hst_read = [0]
        d_cbc_free = [0]
        opj_sp_last = [0]

        emit_A(0, wp1_slots=4)
        while _p1["k"] < len(wtiles):
            emit_wp1()
        emit_wscale()
        emit_B(0, wp1=True)
        emit_C(0, wp2_per_tile=0)
        while _p2["k"] < len(wtiles):
            emit_wp2()
        wp2_tp[0] = _p2["tr"]
        wp2_sp[0] = _p2["ev"]
        emit_D(0)
        emit_A(1)
        emit_B(1)
        emit_C(1)
        emit_D(1)

        finals = [(XQ.h, XQ.n), (WQ.h, WQ.n), (XR.h, XR.n), (WR.h, WR.n), (VP.h, VP.n),
                  (SP.h, SP.n), (TP.h, TP.n), (PQ.h, PQ.n), (QQ.h, QQ.n),
                  (LQ.h, LQ.n), (GQ.h, GQ.n), (SQ.h, SQ.n), (HQ.h, HQ.n),
                  (AQ.h, AQ.n)]

        def fend(e, finals=tuple(finals)):
            for (h, v) in finals:
                e.wait_ge(h, v)
        ops["gpsimd"].append(fend)

        @block.sync
        def _(e):
            for f_ in ops["sync"]:
                f_(e)

        @block.scalar
        def _(e):
            for f_ in ops["scalar"]:
                f_(e)

        @block.vector
        def _(e):
            for f_ in ops["vector"]:
                f_(e)

        @block.tensor
        def _(e):
            for f_ in ops["tensor"]:
                f_(e)

        @block.gpsimd
        def _(e):
            for f_ in ops["gpsimd"]:
                f_(e)

    return nc


_NC_CACHE = [None]


def kernel(L0_attn_hidden_states, L0_attn_w_q, L0_attn_w_k, L0_attn_w_v, L0_attn_w_o,
           L0_ffn_W1, L0_ffn_b1, L0_ffn_W2, L0_ffn_b2):
    x = np.ascontiguousarray(np.asarray(L0_attn_hidden_states, dtype=np.float32))
    wq = np.ascontiguousarray(np.asarray(L0_attn_w_q, dtype=np.float32))
    wk = np.ascontiguousarray(np.asarray(L0_attn_w_k, dtype=np.float32))
    wv = np.ascontiguousarray(np.asarray(L0_attn_w_v, dtype=np.float32))
    wo = np.ascontiguousarray(np.asarray(L0_attn_w_o, dtype=np.float32))
    w1 = np.ascontiguousarray(np.asarray(L0_ffn_W1, dtype=np.float32))
    w2 = np.ascontiguousarray(np.asarray(L0_ffn_W2, dtype=np.float32))
    b1 = np.ascontiguousarray(np.asarray(L0_ffn_b1, dtype=np.float32))
    b2 = np.ascontiguousarray(np.asarray(L0_ffn_b2, dtype=np.float32))
    b1c = np.ascontiguousarray(b1.reshape(NHF, 128).T)
    b2r = np.ascontiguousarray(b2.reshape(1, D)).astype(ml_dtypes.bfloat16)
    mask01, ident, sel16 = make_consts()

    if _NC_CACHE[0] is None:
        _NC_CACHE[0] = build_nc()
    nc = _NC_CACHE[0]

    in_maps = []
    for c in range(NCORE):
        xs = np.ascontiguousarray(x[NSEQ * c:NSEQ * (c + 1)].reshape(NSEQ * S, D))
        in_maps.append({
            "x": xs, "wq": wq, "wk": wk, "wv": wv, "wo": wo,
            "w1": w1, "w2": w2, "b1c": b1c, "b2r": b2r,
            "mask01": mask01, "ident": ident, "sel16": sel16,
        })
    res = run_bass_kernel_spmd(nc, in_maps, core_ids=list(range(NCORE)))
    y0 = np.asarray(res.results[0]["y"])
    out = np.empty((B, S, D), y0.dtype)
    out[0:NSEQ] = y0.reshape(NSEQ, S, D)
    for c in range(1, NCORE):
        out[NSEQ * c:NSEQ * (c + 1)] = np.asarray(res.results[c]["y"]).reshape(NSEQ, S, D)
    return out


# version 14
# speedup vs baseline: 1.0418x; 1.0418x over previous
import numpy as np
from contextlib import ExitStack
import ml_dtypes
import concourse.bass as bass
import concourse.mybir as mybir
from concourse.bass_utils import run_bass_kernel_spmd
from concourse.alu_op_type import AluOpType as OP

F32 = mybir.dt.float32
BF16 = mybir.dt.bfloat16
AFT = mybir.ActivationFunctionType
AX = mybir.AxisListType

B, S, D = 16, 2048, 1024
H, DH = 16, 64
HF = 4096
NCORE = 8
NSEQ = 2
NT = S // 128
ND = D // 128
NHF = HF // 128
NHP = H // 2
TBLK = 1024
NBT = TBLK // 128
NBLK = S // TBLK
M32 = 1.5 * 2.0 ** 23
M64 = 1.5 * 2.0 ** 52
QB = 127.0
EPSQ = 1e-5
EPSLN = 1e-5


def make_consts():
    q = np.arange(128)[None, :]
    m = np.zeros((128, 640), np.float32)
    for j in range(5):
        krow = np.arange(128)[:, None]
        kd = 128 * j + krow - 256
        d = kd - q
        dil = (d != 0) & (d % 4 == 0) & (np.abs(d) <= 256)
        loc = (kd // 64) == (q // 64)
        m[:, 128 * j:128 * (j + 1)] = (dil | loc).astype(np.float32)
    mask01 = m.astype(ml_dtypes.bfloat16)
    ident = np.eye(128, dtype=np.float32)
    sel = np.zeros((16, 2048), np.float32)
    for t in range(16):
        sel[t, 128 * t:128 * (t + 1)] = 1.0
    return mask01, ident, sel


class Ctr:
    def __init__(self, h):
        self.h = h
        self.n = 0


def build_nc():
    nc = bass.Bass("TRN2", target_bir_lowering=False)
    x_h = nc.dram_tensor("x", [NSEQ * S, D], F32, kind="ExternalInput").ap()
    wq_h = nc.dram_tensor("wq", [D, D], F32, kind="ExternalInput").ap()
    wk_h = nc.dram_tensor("wk", [D, D], F32, kind="ExternalInput").ap()
    wv_h = nc.dram_tensor("wv", [D, D], F32, kind="ExternalInput").ap()
    wo_h = nc.dram_tensor("wo", [D, D], F32, kind="ExternalInput").ap()
    w1_h = nc.dram_tensor("w1", [HF, D], F32, kind="ExternalInput").ap()
    w2_h = nc.dram_tensor("w2", [D, HF], F32, kind="ExternalInput").ap()
    b1_h = nc.dram_tensor("b1c", [128, NHF], F32, kind="ExternalInput").ap()
    b2_h = nc.dram_tensor("b2r", [1, D], BF16, kind="ExternalInput").ap()
    mk_h = nc.dram_tensor("mask01", [128, 640], BF16, kind="ExternalInput").ap()
    id_h = nc.dram_tensor("ident", [128, 128], F32, kind="ExternalInput").ap()
    sel_h = nc.dram_tensor("sel16", [16, 2048], F32, kind="ExternalInput").ap()
    y_h = nc.dram_tensor("y", [NSEQ * S, D], F32, kind="ExternalOutput").ap()
    w1qt_h = nc.dram_tensor("w1qt", [NHF, ND, 128, 128], BF16, kind="Internal").ap()
    w2qt_h = nc.dram_tensor("w2qt", [NHF, ND, 128, 128], BF16, kind="Internal").ap()

    ops = {k: [] for k in ("sync", "scalar", "vector", "tensor", "gpsimd")}

    es = ExitStack()
    c_mask = es.enter_context(nc.sbuf_tensor("c_mask", [128, 640], BF16))
    c_id = es.enter_context(nc.sbuf_tensor("c_id", [128, 128], F32))
    c_ones = es.enter_context(nc.sbuf_tensor("c_ones", [128, 128], F32))
    c_onesb = es.enter_context(nc.sbuf_tensor("c_onesb", [1, 128], BF16))
    c_b1 = es.enter_context(nc.sbuf_tensor("c_b1", [128, NHF], F32))
    c_b2 = es.enter_context(nc.sbuf_tensor("c_b2", [1, D], BF16))
    s_det = es.enter_context(nc.sbuf_tensor("s_det", [128, 2], F32))
    s_M = es.enter_context(nc.sbuf_tensor("s_M", [128, 1], F32))
    s_eps = es.enter_context(nc.sbuf_tensor("s_eps", [128, 1], F32))
    s_wsc = es.enter_context(nc.sbuf_tensor("s_wsc", [128, 8], F32))
    s_st0 = es.enter_context(nc.sbuf_tensor("s_st0", [128, 64], F32))
    s_st1 = es.enter_context(nc.sbuf_tensor("s_st1", [128, 64], F32))
    s_wred = s_st1
    s_hst = es.enter_context(nc.sbuf_tensor("s_hst", [128, 24], F32))
    s_hrow = es.enter_context(nc.sbuf_tensor("s_hrow", [16, 128], F32))
    s_r = es.enter_context(nc.sbuf_tensor("s_r", [128, 16], F32))
    s_otm = es.enter_context(nc.sbuf_tensor("s_otm", [128, 1024], BF16))
    s_dump = es.enter_context(nc.sbuf_tensor("s_dump", [128, 1024], BF16))
    a_xnT = es.enter_context(nc.sbuf_tensor("a_xnT", [128, ND, S], BF16))
    a_big = es.enter_context(nc.sbuf_tensor("a_big", [128, 2 * ND * S], BF16))
    a_oT = es.enter_context(nc.sbuf_tensor("a_oT", [128, ND, S], BF16))
    a_vA = es.enter_context(nc.sbuf_tensor("a_vA", [128, 16640], BF16))
    a_aA = es.enter_context(nc.sbuf_tensor("a_aA", [128, 10240], BF16))
    a_aF = es.enter_context(nc.sbuf_tensor("a_aF", [128, 4096], F32))
    ps0 = es.enter_context(nc.psum_tensor("ps0", [128, 512], F32))
    ps1 = es.enter_context(nc.psum_tensor("ps1", [128, 512], F32))
    ps2 = es.enter_context(nc.psum_tensor("ps2", [128, 512], F32))
    ps3 = es.enter_context(nc.psum_tensor("ps3", [128, 512], F32))
    ps4 = es.enter_context(nc.psum_tensor("ps4", [128, 512], F32))
    ps5 = es.enter_context(nc.psum_tensor("ps5", [128, 512], F32))
    ps6 = es.enter_context(nc.psum_tensor("ps6", [128, 512], F32))
    ps7 = es.enter_context(nc.psum_tensor("ps7", [128, 512], F32))
    _XQ = es.enter_context(nc.semaphore("XQ"))
    _XR = es.enter_context(nc.semaphore("XR"))
    _WR = es.enter_context(nc.semaphore("WR"))
    _WQ = es.enter_context(nc.semaphore("WQ"))
    _VP = es.enter_context(nc.semaphore("VP"))
    _SP = es.enter_context(nc.semaphore("SP"))
    _TP = es.enter_context(nc.semaphore("TP"))
    _PQ = es.enter_context(nc.semaphore("PQ"))
    _QQ = es.enter_context(nc.semaphore("QQ"))
    _LQ = es.enter_context(nc.semaphore("LQ"))
    _GQ = es.enter_context(nc.semaphore("GQ"))
    _SQ = es.enter_context(nc.semaphore("SQ"))
    _HQ = es.enter_context(nc.semaphore("HQ"))
    _AQ = es.enter_context(nc.semaphore("AQ"))
    _AR = es.enter_context(nc.semaphore("AR"))
    block = es.enter_context(nc.Block())
    with es:
        PS = [ps0, ps1, ps2, ps3, ps4, ps5, ps6, ps7]
        XQ, WQ, XR, WR = Ctr(_XQ), Ctr(_WQ), Ctr(_XR), Ctr(_WR)
        VP, SP, TP = Ctr(_VP), Ctr(_SP), Ctr(_TP)
        PQ, QQ, LQ, GQ = Ctr(_PQ), Ctr(_QQ), Ctr(_LQ), Ctr(_GQ)
        SQ, HQ, AQ, AR = Ctr(_SQ), Ctr(_HQ), Ctr(_AQ), Ctr(_AR)

        KT = a_big[:, 0:ND * S].rearrange("p (c t) -> p c t", c=ND)
        QT = a_big[:, ND * S:2 * ND * S].rearrange("p (c t) -> p c t", c=ND)
        HB = a_big[:].rearrange("p (c t) -> p c t", c=NHF)
        VTM = a_vA[:].rearrange("p (k c) -> p k c", k=16)
        WO = a_vA[:, 0:ND * 1024].rearrange("p (c n) -> p c n", c=ND)
        W1P = a_vA[:, 8192:8192 + 2 * ND * 128].rearrange("p (b d t) -> p b d t", b=2, d=ND)
        W2C = a_vA[:, 10240:10240 + 2 * 512].rearrange("p (b n) -> p b n", b=2)
        ATN = a_aA[:, 0:16 * 640].rearrange("p (h c) -> p h c", h=16)
        WVH = a_aA[:, 0:ND * 1024].rearrange("p (d n) -> p d n", d=ND)
        WCH = a_aA[:, 8192:8448].rearrange("p (b t) -> p b t", b=2)
        XQS = a_aA[:, 0:ND * 1024].rearrange("p (c t) -> p c t", c=ND)
        CBC = a_aA[:, 8192:9216]
        XQT = a_aF[:, 3072:4096]
        XT = a_aF[:, 0:1024]
        SELD = a_aF[0:8, 2048:3072]
        WQF = a_aF[:, 2048:3072]
        HALF = a_aF[:, 0:1024].rearrange("p (b n) -> p b n", b=2)
        OPX = a_aF[:, 0:1024]
        TMP2 = a_aF[:, 1024:2048]
        STT = [s_st0, s_st1]

        def emit(eng, fn, waits=(), inc=None, k=1):
            wl = tuple((c.h, v) for (c, v) in waits)
            if not wl:
                wl = ((_TP, 0),)
            ip = None
            if inc is not None:
                inc.n += k
                ip = (inc.h, k)

            def g(e, fn=fn, wl=wl, ip=ip):
                for (h, v) in wl:
                    e.wait_ge(h, v)
                instr = fn(e)
                if ip is not None:
                    instr.then_inc(*ip)
            ops[eng].append(g)
            return inc.n if inc is not None else None

        def fc(e):
            e.dma_start(out=c_mask[:], in_=mk_h).then_inc(XQ.h, 16)
            e.dma_start(out=c_id[:], in_=id_h).then_inc(XQ.h, 16)
            e.dma_start(out=c_b1[:], in_=b1_h).then_inc(XQ.h, 16)
            e.dma_start(out=c_b2[:], in_=b2_h).then_inc(XQ.h, 16)
        ops["sync"].append(fc)
        XQ.n += 64
        xq_consts = XQ.n

        emit("vector", lambda e: e.memset(s_det[:, 0:1], 1000.25), inc=VP)
        emit("vector", lambda e: e.tensor_scalar(s_det[:, 0:1], s_det[:, 0:1], M64, M64,
                                                 OP.add, OP.subtract), waits=[(VP, VP.n)], inc=VP)
        emit("vector", lambda e: e.tensor_scalar(s_det[:, 1:2], s_det[:, 0:1], 500.0, None, OP.is_ge),
             waits=[(VP, VP.n)], inc=VP)
        emit("vector", lambda e: e.tensor_scalar(s_det[:, 0:1], s_det[:, 0:1], 500.0, None, OP.is_lt),
             inc=VP)
        emit("vector", lambda e: e.tensor_scalar(s_det[:, 1:2], s_det[:, 1:2], M64, None, OP.mult),
             waits=[(VP, VP.n)], inc=VP)
        emit("vector", lambda e: e.tensor_scalar(s_det[:, 0:1], s_det[:, 0:1], M32, None, OP.mult),
             inc=VP)
        emit("vector", lambda e: e.tensor_add(s_M[:], s_det[:, 0:1], s_det[:, 1:2]),
             waits=[(VP, VP.n)], inc=VP)
        emit("vector", lambda e: e.memset(c_ones[:], 1.0), inc=VP)
        emit("vector", lambda e: e.memset(c_onesb[:], 1.0), inc=VP)
        emit("vector", lambda e: e.memset(s_eps[:], EPSLN), inc=VP)
        ones_cols = VTM[:, :, :].rearrange("p k (h c) -> p k h c", h=16)[:, :, :, 64:65]
        emit("vector", lambda e: e.memset(ones_cols, 1.0), inc=VP)
        vp_pre = VP.n

        wtiles = []
        for i in range(NHF):
            wtiles.append(("w1", w1_h[128 * i:128 * (i + 1), :], i))
        for r in range(8):
            for c in range(4):
                wtiles.append(("w2", w2_h[128 * r:128 * (r + 1), 1024 * c:1024 * (c + 1)], (r, c)))
        wsc_ready = [0]
        wscr_done = [0]
        wt_free = [0]
        _p1 = {"gate": 0, "k": 0, "gp": [0, 0]}
        _p2 = {"gate": 0, "k": 0, "tr": 0, "ev": 0, "trp": [0, 0], "wrp": [0, 0]}

        def emit_wp1():
            if _p1["k"] >= len(wtiles):
                return
            k = _p1["k"]
            _p1["k"] += 1
            kind, apw, _ = wtiles[k]
            par = k % 2
            wbuf = WQF if par == 0 else XQT
            WQc = WQ if par == 0 else WR
            wq_c = emit("sync", lambda e, apw=apw, wbuf=wbuf: e.dma_start(
                out=wbuf[:], in_=apw), waits=[(SP, _p1["gp"][par])], inc=WQc, k=16)
            vpn = emit("scalar", lambda e, k=k, wbuf=wbuf: e.activation(
                wbuf[:], wbuf[:], AFT.Abs, bias=0.0, scale=1.0,
                accum_out=s_wred[:, k:k + 1]), waits=[(WQc, wq_c)], inc=SP)
            _p1["gp"][par] = vpn
            _p1["gate"] = vpn

        def emit_wscale():
            emit("vector", lambda e: e.reduce_sum(s_wsc[:, 4:5], s_wred[:, 0:32], AX.X),
                 waits=[(SP, _p1["gate"])], inc=VP)
            emit("vector", lambda e: e.reduce_sum(s_wsc[:, 5:6], s_wred[:, 32:64], AX.X), inc=VP)

            def ft(e):
                e.matmul(PS[0][0:1, 0:1], c_ones[:, 0:1], s_wsc[:, 4:5], start=True, stop=True)
                return e.matmul(PS[0][0:1, 1:2], c_ones[:, 0:1], s_wsc[:, 5:6], start=True, stop=True)
            emit("tensor", ft, waits=[(VP, VP.n)], inc=TP)
            emit("vector", lambda e: e.tensor_scalar(s_wsc[0:1, 0:2], PS[0][0:1, 0:2],
                                                     1.0 / (HF * D), EPSQ, OP.mult, OP.max),
                 waits=[(TP, TP.n)], inc=VP)
            emit("tensor", lambda e: e.matmul(PS[0][:, 4:6], c_ones[0:1, :], s_wsc[0:1, 0:2],
                                              start=True, stop=True), waits=[(VP, VP.n)], inc=TP)
            emit("vector", lambda e: e.tensor_copy(s_wsc[:, 0:2], PS[0][:, 4:6]),
                 waits=[(TP, TP.n)], inc=VP)
            emit("vector", lambda e: e.reciprocal(s_wsc[:, 2:4], s_wsc[:, 0:2]),
                 waits=[(VP, VP.n)], inc=VP)
            emit("vector", lambda e: e.tensor_scalar(s_wsc[:, 6:7], s_wsc[:, 1:2], 1.0 / QB, None,
                                                     OP.mult), waits=[(VP, VP.n)], inc=VP)
            wsc_ready[0] = VP.n

        def emit_wp2():
            if _p2["k"] >= len(wtiles):
                return
            k = _p2["k"]
            _p2["k"] += 1
            kind, apw, meta = wtiles[k]
            rcol = 2 if kind == "w1" else 3
            par = k % 2
            wbuf = WQF if par == 0 else TMP2
            obuf = s_otm if par == 0 else s_dump
            wdw = [(VP, wsc_ready[0])]
            if _p2["trp"][par]:
                wdw.append((TP, _p2["trp"][par]))
            if par == 1 and k == 1:
                wdw += [(TP, a_tr_last[0]), (SP, xnT_ready[0]), (VP, stats_ready[0])]
            WQc = WQ if par == 0 else WR
            wq_c = emit("sync", lambda e, apw=apw, wbuf=wbuf: e.dma_start(
                out=wbuf[:], in_=apw), waits=wdw, inc=WQc, k=16)
            emit("vector", lambda e, rcol=rcol, wbuf=wbuf: e.tensor_scalar(
                wbuf[:], wbuf[:], s_wsc[:, rcol:rcol + 1], s_M[:],
                OP.mult, OP.add), waits=[(WQc, wq_c)], inc=VP)
            emit("vector", lambda e, wbuf=wbuf: e.tensor_scalar(
                wbuf[:], wbuf[:], s_M[:], 1.0, OP.subtract, OP.min),
                waits=[(VP, VP.n)], inc=VP)
            vq = emit("vector", lambda e, wbuf=wbuf: e.tensor_scalar(
                wbuf[:], wbuf[:], -1.0, None, OP.max), waits=[(VP, VP.n)], inc=VP)

            wtr = [(VP, vq)]
            if _p2["ev"]:
                wtr.append((SP, _p2["ev"]))

            def ftr(e, wbuf=wbuf):
                r = None
                for d2 in range(ND):
                    r = e.transpose(PS[d2 // 4][:, 128 * (d2 % 4):128 * (d2 % 4 + 1)],
                                    wbuf[:, 128 * d2:128 * (d2 + 1)], c_id[:])
                return r
            tpw = emit("tensor", ftr, waits=wtr, inc=TP)
            _p2["trp"][par] = tpw
            _p2["tr"] = tpw

            def fev(e, obuf=obuf):
                r = None
                for d2 in range(ND):
                    r = e.activation(obuf[:, 128 * d2:128 * (d2 + 1)],
                                     PS[d2 // 4][:, 128 * (d2 % 4):128 * (d2 % 4 + 1)],
                                     AFT.Copy, bias=0.0, scale=1.0)
                return r
            sqw = ((16 * (k - 1) + 31) // 32) * 32 if k >= 2 else 0
            spe = emit("scalar", fev, waits=[(TP, tpw), (SQ, sqw)],
                       inc=SP, k=ND)
            _p2["ev"] = spe

            if kind == "w1":
                i = meta

                def fw1(e, i=i, spc=spe, obuf=obuf):
                    e.wait_ge(SP.h, spc)
                    e.dma_start(out=w1qt_h[i, :, :, :].rearrange("d p t -> p d t"),
                                in_=obuf[:].rearrange("p (d t) -> p d t", d=ND)
                                ).then_inc(SQ.h, 16)
                ops["scalar"].append(fw1)
                SQ.n += 16
            else:
                r_, c_ = meta

                def fw2(e, r_=r_, c_=c_, spc=spe, obuf=obuf):
                    e.wait_ge(SP.h, spc)
                    e.dma_start(out=w2qt_h[8 * c_:8 * c_ + 8, r_, :, :]
                                .rearrange("a p t -> p a t"),
                                in_=obuf[:].rearrange("p (a t) -> p a t", a=ND)
                                ).then_inc(SQ.h, 16)
                ops["scalar"].append(fw2)
                SQ.n += 16
            _p2["wrp"][par] = SQ.n
            wscr_done[0] = SQ.n

        stats_ready = [0] * NSEQ
        xnT_ready = [0] * NSEQ
        srow_ready = [0] * NSEQ
        kqv_ready = [0] * NSEQ
        oT_ready = [0] * NSEQ
        oT_tp = [0] * NSEQ
        a_tr_last = [0] * NSEQ
        aF_free = [0]
        aR_free = [0]
        xnT_free = [0]
        d_prev = {"vp": 0, "sp": 0, "tp": 0, "wq": 0}
        oevac_gate = {}

        def emit_A(s, wp1_slots=0):
            st = STT[s]
            tile_done = {}
            mrs_done = {}
            xtr_done = {}
            xdma_free = {}
            for i in range(NT):
                XTb = XT if i % 2 == 0 else TMP2
                row0 = 128 * (NT * s + i)
                base = 48 if i % 2 == 0 else 56
                csum, csq, cvar, crst, cmrs, cxs, ccc = (base + j for j in range(7))
                w = [(XQ, xq_consts)]
                if i >= 2:
                    w += [(VP, xdma_free[i - 2]), (TP, xtr_done[i - 2])]
                if s > 0:
                    w += [(AQ, aF_free[0]), (AR, aR_free[0])]
                XQc = XQ if i % 2 == 0 else XR
                xq_i = emit("sync", lambda e, row0=row0, XTb=XTb: e.dma_start(
                    out=XTb[:], in_=x_h[row0:row0 + 128, :]), waits=w, inc=XQc, k=16)
                wv = [(XQc, xq_i)]
                if i >= 2:
                    wv.append((SP, mrs_done[i - 2]))
                vp_sum = emit("vector", lambda e, csum=csum, XTb=XTb: e.reduce_sum(
                    st[:, csum:csum + 1], XTb[:], AX.X), waits=wv, inc=VP)
                ws = [(XQc, xq_i)]
                if i >= 2:
                    ws.append((VP, tile_done[i - 2]))
                emit("scalar", lambda e, csq=csq, XTb=XTb: e.activation(
                    s_dump[:, :], XTb[:], AFT.Square, bias=0.0, scale=1.0,
                    accum_out=st[:, csq:csq + 1]), waits=ws, inc=SP)
                emit("scalar", lambda e, csum=csum, cvar=cvar: e.activation(
                    st[:, cvar:cvar + 1], st[:, csum:csum + 1], AFT.Square, bias=0.0, scale=1.0 / D),
                    waits=[(VP, vp_sum)], inc=SP)
                sp_a = emit("scalar", lambda e, csum=csum, cmrs=cmrs: e.activation(
                    st[:, cmrs:cmrs + 1], st[:, csum:csum + 1], AFT.Copy, bias=0.0, scale=-1.0 / D),
                    inc=SP)
                vp_var = emit("vector", lambda e, csq=csq, cvar=cvar: e.scalar_tensor_tensor(
                    st[:, cvar:cvar + 1], st[:, csq:csq + 1], 1.0 / D, st[:, cvar:cvar + 1],
                    OP.mult, OP.subtract), waits=[(SP, sp_a)], inc=VP)
                sp_std = emit("scalar", lambda e, cvar=cvar, crst=crst: e.activation(
                    st[:, crst:crst + 1], st[:, cvar:cvar + 1], AFT.Sqrt, bias=s_eps[:], scale=1.0),
                    waits=[(VP, vp_var)], inc=SP)
                vp_rst = emit("vector", lambda e, crst=crst: e.reciprocal(
                    st[:, crst:crst + 1], st[:, crst:crst + 1]), waits=[(SP, sp_std)], inc=VP)
                sp_mrs = emit("scalar", lambda e, cmrs=cmrs, crst=crst: e.activation(
                    st[:, cmrs:cmrs + 1], st[:, cmrs:cmrs + 1], AFT.Copy, bias=0.0,
                    scale=st[:, crst:crst + 1]), waits=[(VP, vp_rst)], inc=SP)
                mrs_done[i] = sp_mrs
                vp_xn = emit("vector", lambda e, crst=crst, cmrs=cmrs, XTb=XTb: e.tensor_scalar(
                    XTb[:], XTb[:], st[:, crst:crst + 1], st[:, cmrs:cmrs + 1],
                    OP.mult, OP.add), waits=[(SP, sp_mrs)], inc=VP)
                vp_xs = emit("vector", lambda e, cxs=cxs, XTb=XTb: e.reduce_max(
                    st[:, cxs:cxs + 1], XTb[:], AX.X, apply_absolute_value=True),
                    waits=[(VP, vp_xn)], inc=VP)
                xdma_free[i] = vp_xs
                emit("vector", lambda e, cxs=cxs, i=i: e.tensor_scalar(
                    st[:, 16 + i:16 + i + 1], st[:, cxs:cxs + 1], EPSQ, 1.0 / QB, OP.max, OP.mult),
                    waits=[(VP, vp_xs)], inc=VP)
                vp_c = emit("vector", lambda e, i=i, ccc=ccc: e.reciprocal(
                    st[:, ccc:ccc + 1], st[:, 16 + i:16 + i + 1]), waits=[(VP, VP.n)], inc=VP)
                vp_rc = emit("vector", lambda e, i=i, crst=crst, ccc=ccc: e.tensor_scalar(
                    st[:, i:i + 1], st[:, crst:crst + 1], st[:, ccc:ccc + 1], None, OP.mult),
                    waits=[(VP, vp_c)], inc=VP)
                vp_mc = emit("vector", lambda e, i=i, cmrs=cmrs, ccc=ccc: e.tensor_scalar(
                    st[:, 32 + i:32 + i + 1], st[:, cmrs:cmrs + 1], st[:, ccc:ccc + 1], None,
                    OP.mult), waits=[(VP, vp_rc), (SP, sp_mrs)], inc=VP)
                tile_done[i] = vp_mc

                def ftr(e, XTb=XTb):
                    r = None
                    for d2 in range(ND):
                        r = e.transpose(PS[4 + d2 // 4][:, 128 * (d2 % 4):128 * (d2 % 4 + 1)],
                                        XTb[:, 128 * d2:128 * (d2 + 1)], c_id[:])
                    return r
                tp_x = emit("tensor", ftr, waits=[(VP, vp_xn)], inc=TP)
                xtr_done[i] = tp_x

                def fxe(e, i=i):
                    r = None
                    for d2 in range(ND):
                        r = e.activation(a_xnT[:, d2, 128 * i:128 * (i + 1)],
                                         PS[4 + d2 // 4][:, 128 * (d2 % 4):128 * (d2 % 4 + 1)],
                                         AFT.Copy, bias=0.0, scale=1.0)
                    return r
                xnev = emit("scalar", fxe, waits=[(TP, tp_x)], inc=SP, k=ND)
                for _ in range(wp1_slots):
                    emit_wp1()
            stats_ready[s] = VP.n
            xnT_ready[s] = xnev
            srow_ready[s] = VP.n
            a_tr_last[s] = xtr_done[NT - 1]

        def emit_B(s, wp1=False):
            gq_base = GQ.n
            chunk_done = {}
            evac_done = {}
            for pi, (wh, dstT) in enumerate(((wq_h, QT), (wk_h, KT))):
                for h8 in range(ND):
                    for d in range(ND):
                        ci = pi * 64 + h8 * ND + d
                        w = []
                        if ci >= 2:
                            w.append((TP, chunk_done[ci - 2]))
                        if s > 0 and ci == 0:
                            w.append((TP, d_prev["tp"]))
                        emit("gpsimd", lambda e, wh=wh, h8=h8, d=d, ci=ci: e.dma_start(
                            out=WCH[:, ci % 2, :],
                            in_=wh[128 * d:128 * (d + 1), 128 * h8:128 * (h8 + 1)]),
                            waits=w, inc=GQ, k=16)
                        wq_c = min(gq_base + 16 * (ci - ci % 2 + 2), gq_base + 16 * 128)
                        wt = [(GQ, wq_c)]
                        if d == 0:
                            wt.append((SP, xnT_ready[s]))
                            key = (pi, h8 - 1) if h8 > 0 else ((pi - 1, ND - 1) if pi > 0 else None)
                            if key is not None:
                                wt.append((VP, evac_done[key]))
                            if s > 0:
                                wt.append((VP, d_prev["vp"]))
                                wt.append((SP, d_prev["sp"]))

                        def fmm(e, ci=ci, d=d):
                            r = None
                            for st4 in range(4):
                                r = e.matmul(PS[4 + st4][:, :], WCH[:, ci % 2, :],
                                             a_xnT[:, d, 512 * st4:512 * (st4 + 1)],
                                             start=(d == 0), stop=(d == ND - 1))
                            return r
                        tpc = emit("tensor", fmm, waits=wt, inc=TP)
                        chunk_done[ci] = tpc
                        if wp1 and ci >= 32:
                            emit_wp2()

                    def fev(e, h8=h8, dstT=dstT):
                        r = None
                        for st4 in range(4):
                            r = e.tensor_copy(dstT[:, h8, 512 * st4:512 * (st4 + 1)], PS[4 + st4][:, :])
                        return r
                    wv = [(TP, TP.n)]
                    if s > 0:
                        wv.append((TP, d_prev["tp"]))
                    ev = emit("vector", fev, waits=wv, inc=VP, k=4)
                    evac_done[(pi, h8)] = ev
                    if wp1:
                        emit_wp1()
            w0 = [(XQ, xq_consts), (TP, chunk_done[127])]
            if wp1 and _p2["ev"]:
                w0.append((SP, _p2["ev"]))
            if s > 0:
                w0.append((VP, d_prev["vp"]))
                w0.append((TP, d_prev["tp"]))

            def fwv(e, wl=tuple((c.h, v) for (c, v) in w0 if v > 0)):
                for (hh, v) in wl:
                    e.wait_ge(hh, v)
                for d in range(ND):
                    e.dma_start(out=WVH[:, d, :], in_=wv_h[128 * d:128 * (d + 1), :]).then_inc(GQ.h, 16)
            ops["gpsimd"].append(fwv)
            GQ.n += ND * 16
            wq_v = GQ.n
            wones = [(VP, VP.n)]
            if s > 0:
                wones += [(VP, d_prev["vp"]), (TP, d_prev["tp"]), (SP, d_prev["sp"])]
            vp_ones = emit("vector", lambda e: e.memset(ones_cols, 1.0), waits=wones, inc=VP)
            v_evac = {}
            for kt in range(16):
                wt = [(GQ, wq_v), (VP, evac_done[(1, ND - 1)]), (VP, srow_ready[s])]
                if kt >= 2:
                    wt.append((VP, v_evac[kt - 2]))
                if s > 0 and kt == 0:
                    wt.append((VP, d_prev["vp"]))
                    wt.append((SP, d_prev["sp"]))

                def fmm(e, kt=kt):
                    r = None
                    for d in range(ND):
                        for half in range(2):
                            r = e.matmul(PS[(kt % 2) * 2 + half][:, :],
                                         a_xnT[:, d, 128 * kt:128 * (kt + 1)],
                                         WVH[:, d, 512 * half:512 * (half + 1)],
                                         start=(d == 0), stop=(d == ND - 1))
                    return r
                tpv = emit("tensor", fmm, waits=wt, inc=TP)

                def fev(e, kt=kt):
                    dst = VTM[:, kt, :].rearrange("p (h c) -> p h c", h=16)[:, :, 0:64]
                    e.tensor_copy(dst[:, 0:8, :],
                                  PS[(kt % 2) * 2][:, :].rearrange("p (h c) -> p h c", h=8))
                    return e.tensor_copy(dst[:, 8:16, :],
                                         PS[(kt % 2) * 2 + 1][:, :].rearrange("p (h c) -> p h c", h=8))
                ev = emit("vector", fev, waits=[(TP, tpv)], inc=VP, k=2)
                v_evac[kt] = ev
                if wp1:
                    emit_wp1()
            kqv_ready[s] = VP.n

        def emit_C(s, wp2_per_tile=0):
            vp_c_entry = VP.n
            otr_done = {}
            exp_done = {}
            mask_done = {}
            oevac_done = {}
            oevac_sp = {}
            atn_gate = {}
            for t in range(NT):
                lo = max(0, 128 * t - 256)
                hi = min(S, 128 * t + 384)
                nkt = (hi - lo) // 128
                kt0 = lo // 128
                j0 = kt0 - (t - 2)
                for hp in range(NHP):
                    par = hp % 2
                    for h01 in range(2):
                        h = 2 * hp + h01
                        pc = PS[(0 if par == 0 else 4) if h01 == 0 else (2 if par == 0 else 5)]
                        pc2 = PS[1 if h01 == 0 else 3][:, 128 * par:128 * (par + 1)]
                        rows = slice(64 * h01, 64 * h01 + 64)
                        w = []
                        if t == 0 and hp == 0 and h01 == 0:
                            w.append((VP, max(kqv_ready[s], vp_c_entry)))
                        if t == 0 and hp < 2 and _p2["ev"]:
                            w.append((SP, _p2["ev"]))
                        if hp >= 2:
                            w.append((SP, exp_done[(t, hp - 2, h01)]))
                        elif t > 0:
                            w.append((VP, oevac_done[t - 1]))
                            w.append((SP, oevac_sp[t - 1]))

                        def fsc(e, t=t, hp=hp, rows=rows, nkt=nkt, kt0=kt0, pc=pc, pc2=pc2):
                            r = None
                            for j in range(nkt):
                                dst = pc[:, 128 * j:128 * (j + 1)] if j < 4 else pc2[:, 0:128]
                                r = e.matmul(dst, KT[rows, hp, 128 * (kt0 + j):128 * (kt0 + j + 1)],
                                             QT[rows, hp, 128 * t:128 * (t + 1)],
                                             start=True, stop=True)
                            return r
                        tpsc = emit("tensor", fsc, waits=w, inc=TP)
                        n1 = min(nkt, 4) * 128
                        wx = [(TP, tpsc)]
                        ag = atn_gate.get((t, h), 0)
                        if ag:
                            wx.append((TP, ag))

                        def fex(e, h=h, pc=pc, pc2=pc2, nkt=nkt, n1=n1):
                            r = e.activation(ATN[:, h, 0:n1], pc[:, 0:n1], AFT.Exp,
                                             bias=0.0, scale=0.125)
                            if nkt > 4:
                                r = e.activation(ATN[:, h, 512:640], pc2[:, 0:128], AFT.Exp,
                                                 bias=0.0, scale=0.125)
                            return r
                        spx = emit("scalar", fex, waits=wx, inc=SP)
                        exp_done[(t, hp, h01)] = spx
                        vpm = emit("vector", lambda e, h=h, nkt=nkt, j0=j0: e.tensor_tensor(
                            ATN[:, h, 0:128 * nkt], ATN[:, h, 0:128 * nkt],
                            c_mask[:, 128 * j0:128 * (j0 + nkt)], OP.mult),
                            waits=[(SP, spx)], inc=VP)
                        mask_done[(t, h)] = vpm
                wav = [(VP, max(mask_done[(t, h)] for h in range(16)))]
                if t > 0:
                    wav.append((VP, oevac_done[t - 1]))
                    wav.append((SP, oevac_sp[t - 1]))

                def fav(e, nkt=nkt, kt0=kt0):
                    r = None
                    for hp in range(NHP):
                        for h01 in range(2):
                            h = 2 * hp + h01
                            c0 = 256 * (hp % 2) + 65 * h01
                            dst = PS[hp // 2][:, c0:c0 + 65]
                            for j in range(nkt):
                                r = e.matmul(dst, ATN[:, h, 128 * j:128 * (j + 1)],
                                             VTM[:, kt0 + j, 65 * h:65 * h + 65],
                                             start=(j == 0), stop=(j == nkt - 1))
                    return r
                tpav = emit("tensor", fav, waits=wav, inc=TP)
                oevac_gate[t] = tpav
                for h in range(16):
                    atn_gate[(t + 1, h)] = tpav

                def fr(e):
                    r = None
                    for qq in range(4):
                        v2 = PS[qq][:, :].rearrange("p (a c) -> p a c", a=2)
                        r = e.reciprocal(s_r[:, 4 * qq + 0:4 * qq + 2],
                                         v2[:, :, 64:65].rearrange("p a c -> p (a c)"))
                        r = e.reciprocal(s_r[:, 4 * qq + 2:4 * qq + 4],
                                         v2[:, :, 129:130].rearrange("p a c -> p (a c)"))
                    return r
                vpr = emit("vector", fr, waits=[(TP, tpav)], inc=VP, k=8)

                def foe(e):
                    r = None
                    for hp in range(NHP):
                        for h01 in range(2):
                            h = 2 * hp + h01
                            c0 = 256 * (hp % 2) + 65 * h01
                            rcol = 4 * (hp // 2) + (hp % 2) + 2 * h01
                            r = e.tensor_scalar(XQT[:, 64 * h:64 * (h + 1)],
                                                PS[hp // 2][:, c0:c0 + 64],
                                                s_r[:, rcol:rcol + 1], None, OP.mult)
                    return r
                woe = [(VP, vpr)]
                if t >= 1:
                    woe.append((TP, otr_done[t - 1]))
                vpoe = emit("vector", foe, waits=woe, inc=VP, k=16)
                oevac_done[t] = vpoe

                def ftr(e):
                    r = None
                    for c in range(ND):
                        r = e.transpose(PS[c // 4][:, 128 * (c % 4):128 * (c % 4 + 1)],
                                        XQT[:, 128 * c:128 * (c + 1)], c_id[:])
                    return r
                wtro = [(VP, vpoe)]
                if t >= 1:
                    wtro.append((SP, oevac_sp[t - 1]))
                tp_o = emit("tensor", ftr, waits=wtro, inc=TP)
                otr_done[t] = tp_o

                def foev(e, t=t):
                    r = None
                    for c in range(ND):
                        r = e.activation(a_oT[:, c, 128 * t:128 * (t + 1)],
                                         PS[c // 4][:, 128 * (c % 4):128 * (c % 4 + 1)],
                                         AFT.Copy, bias=0.0, scale=1.0)
                    return r
                oevac_sp[t] = emit("scalar", foev, waits=[(TP, tp_o)], inc=SP, k=ND)
            oT_ready[s] = oevac_sp[NT - 1]
            oT_tp[s] = otr_done[NT - 1]

        def emit_D(s):
            st_ = STT[s]
            def fwo(e, need=oevac_gate[NT - 1]):
                e.wait_ge(TP.h, need)
                for c in range(ND):
                    e.dma_start(out=WO[:, c, :], in_=wo_h[128 * c:128 * (c + 1), :]).then_inc(GQ.h, 16)
            ops["gpsimd"].append(fwo)
            GQ.n += ND * 16
            wq_wo = GQ.n
            gelu_gate = [0]
            opj_copy = {}
            halfev_vp = {}
            half_written = {}
            for b in range(NBLK):
                if b == 0:
                    def gsel(e, need=max(sel_last_tp[0], oevac_gate[NT - 1], wp2_tp[0])):
                        e.wait_ge(TP.h, need)
                        e.dma_start(out=SELD, in_=sel_h[0:8, 0:1024]).then_inc(LQ.h, 16)
                    ops["sync"].append(gsel)
                    LQ.n += 16
                xq_p3 = {}
                xqtr = {}
                sp_ev = {}
                for j in range(NBT):
                    t = NBT * b + j
                    row0 = 128 * (NT * s + t)
                    wxd = [(VP, xq_p3.get(j - 1, d_xq_gate[0])), (TP, d_pmaxtr[0]),
                           (XQ, XQ.n), (AR, tmp2slot[1])]
                    if j >= 1:
                        wxd.append((TP, xqtr[j - 1]))
                    elif b == 0:
                        wxd.append((TP, oT_tp[s]))
                    else:
                        wxd.append((TP, xqtr_prev[0]))
                    xd = emit("sync", lambda e, row0=row0: e.dma_start(
                        out=XQT[:, :], in_=x_h[row0:row0 + 128, :]), waits=wxd, inc=XQ, k=16)
                    p1 = emit("vector", lambda e, t=t: e.tensor_scalar(
                        XQT[:, :], XQT[:, :], st_[:, t:t + 1], st_[:, 32 + t:33 + t],
                        OP.mult, OP.add), waits=[(XQ, xd), (VP, stats_ready[s])], inc=VP)
                    p2 = emit("vector", lambda e: e.tensor_scalar(
                        XQT[:, :], XQT[:, :], s_M[:], s_M[:], OP.add, OP.subtract),
                        waits=[(VP, p1)], inc=VP)
                    p3 = emit("vector", lambda e, t=t: e.tensor_scalar(
                        XQT[:, :], XQT[:, :], st_[:, 16 + t:17 + t], None, OP.mult),
                        waits=[(VP, p2)], inc=VP)
                    xq_p3[j] = p3

                    def ftq(e):
                        r = None
                        for c2 in range(ND):
                            r = e.transpose(PS[6 + c2 // 4][:, 128 * (c2 % 4):128 * (c2 % 4 + 1)],
                                            XQT[:, 128 * c2:128 * (c2 + 1)], c_id[:])
                        return r
                    wtq_ = [(VP, p3)]
                    if j >= 1:
                        wtq_.append((SP, sp_ev[j - 1]))
                    else:
                        wtq_.append((SP, max(wp2_sp[0], xq_sp_prev[0])))
                    tp_q = emit("tensor", ftq, waits=wtq_, inc=TP)
                    xqtr[j] = tp_q

                    def fqe(e, j=j):
                        r = None
                        for c2 in range(ND):
                            r = e.activation(XQS[:, c2, 128 * j:128 * (j + 1)],
                                             PS[6 + c2 // 4][:, 128 * (c2 % 4):128 * (c2 % 4 + 1)],
                                             AFT.Copy, bias=0.0, scale=1.0)
                        return r
                    xq_done_sp = emit("scalar", fqe, waits=[(TP, tp_q)], inc=SP, k=ND)
                    sp_ev[j] = xq_done_sp
                xqtr_prev[0] = xqtr[NBT - 1]
                xq_sp_prev[0] = xq_done_sp
                xq_done_tp = xqtr[NBT - 1]
                d_xq_gate[0] = xq_p3[NBT - 1]
                pq_base = PQ.n
                panel_done = {}
                sp_gelu = {}
                for hc in range(NHF):
                    w = [(VP, wsc_ready[0]), (SQ, wscr_done[0]), (TP, w1slot[hc % 2])]
                    if hc < 2:
                        w.append((TP, oevac_gate[NT - 1]))
                    emit("sync", lambda e, hc=hc: e.dma_start(
                        out=W1P[:, hc % 2, :, :],
                        in_=w1qt_h[hc, :, :, :].rearrange("d p t -> p d t")), waits=w, inc=PQ, k=16)
                    xq_p = min(pq_base + 16 * (hc - hc % 2 + 2), pq_base + 16 * NHF)
                    wt = [(PQ, xq_p)]
                    if hc == 0:
                        wt += [(SP, xq_done_sp), (TP, gelu_gate[0])]
                        if b > 0:
                            wt += [(SP, opj_sp_last[0])]
                    if hc >= 2:
                        wt.append((SP, sp_gelu[hc - 2]))

                    def fm1(e, hc=hc):
                        r = None
                        for half in range(2):
                            for d in range(ND):
                                r = e.matmul(PS[2 * (hc % 2) + half][:, :], W1P[:, hc % 2, d, :],
                                             XQS[:, d, 512 * half:512 * (half + 1)],
                                             start=(d == 0), stop=(d == ND - 1))
                        return r
                    tp1 = emit("tensor", fm1, waits=wt, inc=TP)
                    panel_done[hc] = tp1
                    w1slot[hc % 2] = tp1
                    d_ffn1_last[0] = tp1

                    def fg(e, hc=hc):
                        r = None
                        for half in range(2):
                            r = e.activation(HB[:, hc, 512 * half:512 * (half + 1)],
                                             PS[2 * (hc % 2) + half][:, :], AFT.Gelu,
                                             bias=c_b1[:, hc:hc + 1], scale=s_wsc[:, 0:1])
                        return r
                    spg = emit("scalar", fg, waits=[(TP, tp1)], inc=SP)
                    sp_gelu[hc] = spg
                    if hc == 1:
                        emit("vector", lambda e: e.tensor_tensor(
                            s_dump[:, :], HB[:, 0, :], HB[:, 1, :], OP.max),
                            waits=[(SP, spg)], inc=VP)
                        emit("vector", lambda e: e.tensor_tensor(
                            s_otm[:, :], HB[:, 0, :], HB[:, 1, :], OP.min),
                            waits=[(VP, VP.n)], inc=VP)
                    elif hc >= 2:
                        emit("vector", lambda e, hc=hc: e.tensor_tensor(
                            s_dump[:, :], s_dump[:, :], HB[:, hc, :], OP.max),
                            waits=[(SP, spg), (VP, VP.n)], inc=VP)
                        emit("vector", lambda e, hc=hc: e.tensor_tensor(
                            s_otm[:, :], s_otm[:, :], HB[:, hc, :], OP.min),
                            waits=[(VP, VP.n)], inc=VP)
                emit("vector", lambda e: e.tensor_scalar(
                    s_otm[:, :], s_otm[:, :], -1.0, None, OP.mult), waits=[(VP, VP.n)], inc=VP)
                emit("vector", lambda e: e.tensor_tensor(
                    XQT[:, :], s_dump[:, :], s_otm[:, :], OP.max),
                    waits=[(VP, VP.n), (AR, tmp2slot[1])], inc=VP)
                vp_tree = VP.n
                for j in range(NBT):
                    tpt = emit("tensor", lambda e, j=j: e.transpose(
                        PS[4 + (j % 2)][:, 0:128],
                        XQT[:, 128 * j:128 * (j + 1)], c_id[:]),
                        waits=[(VP, vp_tree)] + ([(VP, hst_read[0])] if j >= 2 else []), inc=TP)
                    d_pmaxtr[0] = tpt
                    vpr1 = emit("vector", lambda e, j=j: e.reduce_max(
                        s_hst[:, j:j + 1], PS[4 + (j % 2)][:, 0:128], AX.X,
                        apply_absolute_value=True), waits=[(TP, tpt)], inc=VP)
                    hst_read[0] = vpr1
                    emit("vector", lambda e, j=j: e.tensor_scalar(
                        s_hst[:, j:j + 1], s_hst[:, j:j + 1], EPSQ, None, OP.max),
                        waits=[(VP, VP.n)], inc=VP)
                    emit("vector", lambda e, j=j: e.tensor_scalar(
                        s_hst[:, 8 + j:9 + j], s_hst[:, j:j + 1], s_wsc[:, 6:7], None, OP.mult),
                        waits=[(VP, VP.n)], inc=VP)
                    emit("vector", lambda e, j=j: e.reciprocal(
                        s_hst[:, 16 + j:17 + j], s_hst[:, j:j + 1]), waits=[(VP, VP.n - 1)], inc=VP)
                    emit("vector", lambda e, j=j: e.tensor_scalar(
                        s_hst[:, 16 + j:17 + j], s_hst[:, 16 + j:17 + j], QB, None, OP.mult),
                        waits=[(VP, VP.n)], inc=VP)
                vp_hst = VP.n
                tph = emit("tensor", lambda e: e.transpose(PS[4][0:8, 0:128], s_hst[:, 16:24],
                                                           c_id[:]),
                           waits=[(VP, vp_hst)], inc=TP)
                vph = emit("vector", lambda e: e.tensor_copy(s_hrow[0:8, 0:128], PS[4][0:8, 0:128]),
                           waits=[(TP, tph)], inc=VP)

                if b > 0:
                    def gsel0(e, need=TP.n):
                        e.wait_ge(TP.h, need)
                        e.dma_start(out=SELD, in_=sel_h[0:8, 0:1024]).then_inc(LQ.h, 16)
                    ops["sync"].append(gsel0)
                    LQ.n += 16
                xq_sel0 = LQ.n

                def fmhb(e, vph=vph, xq_sel0=xq_sel0, hr=hst_read[0]):
                    e.wait_ge(LQ.h, xq_sel0)
                    e.wait_ge(VP.h, vph)
                    e.wait_ge(VP.h, hr)
                    r = None
                    for j in range(NBT):
                        r = e.matmul(PS[4 + j // 4][:, 128 * (j % 4):128 * (j % 4 + 1)],
                                     SELD[:, 128 * j:128 * (j + 1)], s_hrow[0:8, 0:128],
                                     start=True, stop=True)
                    return r
                tpm = emit("tensor", fmhb, waits=[], inc=TP)
                sel_last_tp[0] = tpm
                vphb = emit("vector", lambda e: e.tensor_copy(CBC.bitcast(BF16)[:, 0:512], PS[4][:, :]),
                            waits=[(TP, TP.n)], inc=VP)
                vphb = emit("vector", lambda e: e.tensor_copy(CBC.bitcast(BF16)[:, 512:1024],
                                                              PS[5][:, :]), inc=VP)
                d_cbc_free[0] = 0
                for hc in range(NHF):
                    emit("vector", lambda e, hc=hc: e.tensor_tensor(
                        HB[:, hc, :], HB[:, hc, :], CBC.bitcast(BF16)[:, :], OP.mult),
                        waits=[(VP, vphb)], inc=VP)
                    emit("vector", lambda e, hc=hc: e.tensor_scalar(
                        HB[:, hc, :], HB[:, hc, :], s_M[:], s_M[:], OP.add, OP.subtract),
                        waits=[(VP, VP.n)], inc=VP)
                vp_hq = VP.n
                d_cbc_free[0] = vp_hq
                for dmh in range(2):
                    qq_base = QQ.n
                    ch_done = {}
                    for hc in range(NHF):
                        w = [(SQ, wscr_done[0]), (TP, w2slot[hc % 2])]
                        if hc < 2:
                            w.append((TP, oevac_gate[NT - 1]))
                        emit("sync", lambda e, hc=hc, dmh=dmh: e.dma_start(
                            out=W2C[:, hc % 2, :].rearrange("p (a t) -> p a t", a=4),
                            in_=w2qt_h[hc, 4 * dmh:4 * dmh + 4, :, :].rearrange("a p t -> p a t")),
                            waits=w, inc=QQ, k=16)
                        xq2 = min(qq_base + 16 * (hc - hc % 2 + 2), qq_base + 16 * NHF)
                        wt = [(QQ, xq2)]
                        if hc == 0:
                            wt.append((VP, vp_hq))
                            if dmh == 1:
                                wt.append((VP, halfev_vp[(b, 0, NBT - 1)]))

                        def fm2(e, hc=hc):
                            r = None
                            for j in range(NBT):
                                r = e.matmul(PS[j][:, :], HB[:, hc, 128 * j:128 * (j + 1)],
                                             W2C[:, hc % 2, :], start=(hc == 0), stop=(hc == NHF - 1))
                            return r
                        tp2 = emit("tensor", fm2, waits=wt, inc=TP)
                        ch_done[hc] = tp2
                        w2slot[hc % 2] = tp2
                    tp_f2 = TP.n
                    for j in range(NBT):
                        t = NBT * b + j
                        row0 = 128 * (NT * s + t)
                        w = [(TP, tp_f2), (HQ, halfslot[j % 2]), (VP, d_opx[0])]
                        hv = emit("vector", lambda e, j=j: e.tensor_scalar(
                            HALF[:, j % 2, :], PS[j][:, :], s_hst[:, 8 + j:9 + j], None, OP.mult),
                            waits=w, inc=VP)
                        halfev_vp[(b, dmh, j)] = hv
                        hw = emit("gpsimd", lambda e, j=j, row0=row0, dmh=dmh: e.dma_start(
                            out=y_h[row0:row0 + 128, 512 * dmh:512 * (dmh + 1)],
                            in_=HALF[:, j % 2, :]), waits=[(VP, hv), (HQ, HQ.n)], inc=HQ, k=16)
                        half_written[(dmh, j)] = hw
                        halfslot[j % 2] = hw
                gelu_gate[0] = TP.n
                for j in range(NBT):
                    t = NBT * b + j
                    row0 = 128 * (NT * s + t)
                    pa, pb = PS[2 * (j % 2)], PS[2 * (j % 2) + 1]
                    w = [(GQ, wq_wo), (SP, oT_ready[s]),
                         (VP, halfev_vp[(b, 1, 2 * (j % 2))]),
                         (VP, halfev_vp[(b, 1, 2 * (j % 2) + 1)]), (XQ, xq_consts)]
                    if j >= 2:
                        w.append((SP, opj_copy[j - 2]))

                    def fop(e, t=t, pa=pa, pb=pb):
                        e.matmul(pa[:, :], c_onesb[0:1, :], c_b2[0:1, 0:512], start=True, stop=False)
                        e.matmul(pb[:, :], c_onesb[0:1, :], c_b2[0:1, 512:1024], start=True, stop=False)
                        r = None
                        for c in range(ND):
                            e.matmul(pa[:, :], a_oT[:, c, 128 * t:128 * (t + 1)],
                                     WO[:, c, 0:512], start=False, stop=(c == ND - 1))
                            r = e.matmul(pb[:, :], a_oT[:, c, 128 * t:128 * (t + 1)],
                                         WO[:, c, 512:1024], start=False, stop=(c == ND - 1))
                        return r
                    tpo = emit("tensor", fop, waits=w, inc=TP)

                    OB = TMP2 if j % 2 == 0 else XQT
                    AQc = AQ if j % 2 == 0 else AR

                    def fcp(e, pa=pa, pb=pb, OB=OB):
                        e.copy(OB[:, 0:512], pa[:, :])
                        return e.copy(OB[:, 512:1024], pb[:, :])
                    spc = emit("scalar", fcp, waits=[(TP, tpo), (AQc, tmp2slot[j % 2])],
                               inc=SP, k=2)
                    opj_copy[j] = spc
                    wyr = [(SP, spc), (HQ, HQ.n),
                           (VP, max(d_opx[0], halfev_vp[(b, 1, NBT - 1)])), (XQ, XQ.n)]
                    yr = emit("sync", lambda e, row0=row0: e.dma_start(
                        out=OPX, in_=y_h[row0:row0 + 128, :]), waits=wyr, inc=XQ, k=16)
                    a1 = emit("vector", lambda e, OB=OB: e.tensor_tensor(
                        OB[:, :], OB[:, :], OPX, OP.add),
                        waits=[(XQ, yr), (SP, spc)], inc=VP)
                    xr = emit("sync", lambda e, row0=row0: e.dma_start(
                        out=OPX, in_=x_h[row0:row0 + 128, :]), waits=[(VP, a1)], inc=XQ, k=16)
                    a2 = emit("vector", lambda e, OB=OB: e.tensor_tensor(
                        OB[:, :], OB[:, :], OPX, OP.add),
                        waits=[(XQ, xr), (VP, a1)], inc=VP)
                    d_opx[0] = a2
                    fin = emit("gpsimd", lambda e, row0=row0, OB=OB, AQc=AQc: e.dma_start(
                        out=y_h[row0:row0 + 128, :], in_=OB[:, :]),
                        waits=[(VP, a2), (AQc, AQc.n)], inc=AQc, k=16)
                    tmp2slot[j % 2] = fin
                opj_sp_last[0] = opj_copy[NBT - 1]
            d_prev["vp"], d_prev["sp"], d_prev["tp"], d_prev["wq"] = VP.n, SP.n, TP.n, AQ.n
            aF_free[0] = AQ.n
            aR_free[0] = AR.n

        selbuf_gate = {}
        d_xq_gate = [0]
        d_pmaxtr = [0]
        d_opx = [0]
        xqtr_prev = [0]
        xq_sp_prev = [0]
        d_ffn1_last = [0]
        w1slot = [0, 0]
        w2slot = [0, 0]
        halfslot = [0, 0]
        tmp2slot = [0, 0]
        sel_last_tp = [0]
        wp2_tp = [0]
        wp2_sp = [0]
        spb_free = [0]
        hst_read = [0]
        d_cbc_free = [0]
        opj_sp_last = [0]

        emit_A(0, wp1_slots=4)
        while _p1["k"] < len(wtiles):
            emit_wp1()
        emit_wscale()
        emit_B(0, wp1=True)
        emit_C(0, wp2_per_tile=0)
        while _p2["k"] < len(wtiles):
            emit_wp2()
        wp2_tp[0] = _p2["tr"]
        wp2_sp[0] = _p2["ev"]
        emit_D(0)
        emit_A(1)
        emit_B(1)
        emit_C(1)
        emit_D(1)

        finals = [(XQ.h, XQ.n), (WQ.h, WQ.n), (XR.h, XR.n), (WR.h, WR.n), (VP.h, VP.n),
                  (SP.h, SP.n), (TP.h, TP.n), (PQ.h, PQ.n), (QQ.h, QQ.n),
                  (LQ.h, LQ.n), (GQ.h, GQ.n), (SQ.h, SQ.n), (HQ.h, HQ.n),
                  (AQ.h, AQ.n), (AR.h, AR.n)]

        def fend(e, finals=tuple(finals)):
            for (h, v) in finals:
                e.wait_ge(h, v)
        ops["gpsimd"].append(fend)

        @block.sync
        def _(e):
            for f_ in ops["sync"]:
                f_(e)

        @block.scalar
        def _(e):
            for f_ in ops["scalar"]:
                f_(e)

        @block.vector
        def _(e):
            for f_ in ops["vector"]:
                f_(e)

        @block.tensor
        def _(e):
            for f_ in ops["tensor"]:
                f_(e)

        @block.gpsimd
        def _(e):
            for f_ in ops["gpsimd"]:
                f_(e)

    return nc


_NC_CACHE = [None]


def kernel(L0_attn_hidden_states, L0_attn_w_q, L0_attn_w_k, L0_attn_w_v, L0_attn_w_o,
           L0_ffn_W1, L0_ffn_b1, L0_ffn_W2, L0_ffn_b2):
    x = np.ascontiguousarray(np.asarray(L0_attn_hidden_states, dtype=np.float32))
    wq = np.ascontiguousarray(np.asarray(L0_attn_w_q, dtype=np.float32))
    wk = np.ascontiguousarray(np.asarray(L0_attn_w_k, dtype=np.float32))
    wv = np.ascontiguousarray(np.asarray(L0_attn_w_v, dtype=np.float32))
    wo = np.ascontiguousarray(np.asarray(L0_attn_w_o, dtype=np.float32))
    w1 = np.ascontiguousarray(np.asarray(L0_ffn_W1, dtype=np.float32))
    w2 = np.ascontiguousarray(np.asarray(L0_ffn_W2, dtype=np.float32))
    b1 = np.ascontiguousarray(np.asarray(L0_ffn_b1, dtype=np.float32))
    b2 = np.ascontiguousarray(np.asarray(L0_ffn_b2, dtype=np.float32))
    b1c = np.ascontiguousarray(b1.reshape(NHF, 128).T)
    b2r = np.ascontiguousarray(b2.reshape(1, D)).astype(ml_dtypes.bfloat16)
    mask01, ident, sel16 = make_consts()

    if _NC_CACHE[0] is None:
        _NC_CACHE[0] = build_nc()
    nc = _NC_CACHE[0]

    in_maps = []
    for c in range(NCORE):
        xs = np.ascontiguousarray(x[NSEQ * c:NSEQ * (c + 1)].reshape(NSEQ * S, D))
        in_maps.append({
            "x": xs, "wq": wq, "wk": wk, "wv": wv, "wo": wo,
            "w1": w1, "w2": w2, "b1c": b1c, "b2r": b2r,
            "mask01": mask01, "ident": ident, "sel16": sel16,
        })
    res = run_bass_kernel_spmd(nc, in_maps, core_ids=list(range(NCORE)))
    y0 = np.asarray(res.results[0]["y"])
    out = np.empty((B, S, D), y0.dtype)
    out[0:NSEQ] = y0.reshape(NSEQ, S, D)
    for c in range(1, NCORE):
        out[NSEQ * c:NSEQ * (c + 1)] = np.asarray(res.results[c]["y"]).reshape(NSEQ, S, D)
    return out


# version 15 (final)
# speedup vs baseline: 1.0575x; 1.0151x over previous
import numpy as np
from contextlib import ExitStack
import ml_dtypes
import concourse.bass as bass
import concourse.mybir as mybir
from concourse.bass_utils import run_bass_kernel_spmd
from concourse.alu_op_type import AluOpType as OP

F32 = mybir.dt.float32
BF16 = mybir.dt.bfloat16
AFT = mybir.ActivationFunctionType
AX = mybir.AxisListType

B, S, D = 16, 2048, 1024
H, DH = 16, 64
HF = 4096
NCORE = 8
NSEQ = 2
NT = S // 128
ND = D // 128
NHF = HF // 128
NHP = H // 2
TBLK = 1024
NBT = TBLK // 128
NBLK = S // TBLK
M32 = 1.5 * 2.0 ** 23
M64 = 1.5 * 2.0 ** 52
QB = 127.0
EPSQ = 1e-5
EPSLN = 1e-5


def make_consts():
    q = np.arange(128)[None, :]
    m = np.zeros((128, 640), np.float32)
    for j in range(5):
        krow = np.arange(128)[:, None]
        kd = 128 * j + krow - 256
        d = kd - q
        dil = (d != 0) & (d % 4 == 0) & (np.abs(d) <= 256)
        loc = (kd // 64) == (q // 64)
        m[:, 128 * j:128 * (j + 1)] = (dil | loc).astype(np.float32)
    mask01 = m.astype(ml_dtypes.bfloat16)
    ident = np.eye(128, dtype=np.float32)
    sel = np.zeros((16, 2048), np.float32)
    for t in range(16):
        sel[t, 128 * t:128 * (t + 1)] = 1.0
    return mask01, ident, sel


class Ctr:
    def __init__(self, h):
        self.h = h
        self.n = 0


def build_nc():
    nc = bass.Bass("TRN2", target_bir_lowering=False)
    x_h = nc.dram_tensor("x", [NSEQ * S, D], F32, kind="ExternalInput").ap()
    wq_h = nc.dram_tensor("wq", [D, D], F32, kind="ExternalInput").ap()
    wk_h = nc.dram_tensor("wk", [D, D], F32, kind="ExternalInput").ap()
    wv_h = nc.dram_tensor("wv", [D, D], F32, kind="ExternalInput").ap()
    wo_h = nc.dram_tensor("wo", [D, D], F32, kind="ExternalInput").ap()
    w1_h = nc.dram_tensor("w1", [HF, D], F32, kind="ExternalInput").ap()
    w2_h = nc.dram_tensor("w2", [D, HF], F32, kind="ExternalInput").ap()
    b1_h = nc.dram_tensor("b1c", [128, NHF], F32, kind="ExternalInput").ap()
    b2_h = nc.dram_tensor("b2r", [1, D], BF16, kind="ExternalInput").ap()
    mk_h = nc.dram_tensor("mask01", [128, 640], BF16, kind="ExternalInput").ap()
    id_h = nc.dram_tensor("ident", [128, 128], F32, kind="ExternalInput").ap()
    sel_h = nc.dram_tensor("sel16", [16, 2048], F32, kind="ExternalInput").ap()
    y_h = nc.dram_tensor("y", [NSEQ * S, D], F32, kind="ExternalOutput").ap()
    w1qt_h = nc.dram_tensor("w1qt", [NHF, ND, 128, 128], BF16, kind="Internal").ap()
    w2qt_h = nc.dram_tensor("w2qt", [NHF, ND, 128, 128], BF16, kind="Internal").ap()

    ops = {k: [] for k in ("sync", "scalar", "vector", "tensor", "gpsimd")}

    es = ExitStack()
    c_mask = es.enter_context(nc.sbuf_tensor("c_mask", [128, 640], BF16))
    c_id = es.enter_context(nc.sbuf_tensor("c_id", [128, 128], F32))
    c_ones = es.enter_context(nc.sbuf_tensor("c_ones", [128, 128], F32))
    c_onesb = es.enter_context(nc.sbuf_tensor("c_onesb", [1, 128], BF16))
    c_b1 = es.enter_context(nc.sbuf_tensor("c_b1", [128, NHF], F32))
    c_b2 = es.enter_context(nc.sbuf_tensor("c_b2", [1, D], BF16))
    s_det = es.enter_context(nc.sbuf_tensor("s_det", [128, 2], F32))
    s_M = es.enter_context(nc.sbuf_tensor("s_M", [128, 1], F32))
    s_eps = es.enter_context(nc.sbuf_tensor("s_eps", [128, 1], F32))
    s_wsc = es.enter_context(nc.sbuf_tensor("s_wsc", [128, 8], F32))
    s_st0 = es.enter_context(nc.sbuf_tensor("s_st0", [128, 64], F32))
    s_st1 = es.enter_context(nc.sbuf_tensor("s_st1", [128, 64], F32))
    s_wred = s_st1
    s_hst = es.enter_context(nc.sbuf_tensor("s_hst", [128, 24], F32))
    s_hrow = es.enter_context(nc.sbuf_tensor("s_hrow", [16, 128], F32))
    s_r = es.enter_context(nc.sbuf_tensor("s_r", [128, 16], F32))
    s_otm = es.enter_context(nc.sbuf_tensor("s_otm", [128, 1024], BF16))
    s_dump = es.enter_context(nc.sbuf_tensor("s_dump", [128, 1024], BF16))
    a_xnT = es.enter_context(nc.sbuf_tensor("a_xnT", [128, ND, S], BF16))
    a_big = es.enter_context(nc.sbuf_tensor("a_big", [128, 2 * ND * S], BF16))
    a_oT = es.enter_context(nc.sbuf_tensor("a_oT", [128, ND, S], BF16))
    a_vA = es.enter_context(nc.sbuf_tensor("a_vA", [128, 16640], BF16))
    a_aA = es.enter_context(nc.sbuf_tensor("a_aA", [128, 10240], BF16))
    a_aF = es.enter_context(nc.sbuf_tensor("a_aF", [128, 4096], F32))
    ps0 = es.enter_context(nc.psum_tensor("ps0", [128, 512], F32))
    ps1 = es.enter_context(nc.psum_tensor("ps1", [128, 512], F32))
    ps2 = es.enter_context(nc.psum_tensor("ps2", [128, 512], F32))
    ps3 = es.enter_context(nc.psum_tensor("ps3", [128, 512], F32))
    ps4 = es.enter_context(nc.psum_tensor("ps4", [128, 512], F32))
    ps5 = es.enter_context(nc.psum_tensor("ps5", [128, 512], F32))
    ps6 = es.enter_context(nc.psum_tensor("ps6", [128, 512], F32))
    ps7 = es.enter_context(nc.psum_tensor("ps7", [128, 512], F32))
    _XQ = es.enter_context(nc.semaphore("XQ"))
    _XR = es.enter_context(nc.semaphore("XR"))
    _WR = es.enter_context(nc.semaphore("WR"))
    _WQ = es.enter_context(nc.semaphore("WQ"))
    _VP = es.enter_context(nc.semaphore("VP"))
    _SP = es.enter_context(nc.semaphore("SP"))
    _TP = es.enter_context(nc.semaphore("TP"))
    _PQ = es.enter_context(nc.semaphore("PQ"))
    _QQ = es.enter_context(nc.semaphore("QQ"))
    _LQ = es.enter_context(nc.semaphore("LQ"))
    _GQ = es.enter_context(nc.semaphore("GQ"))
    _SQ = es.enter_context(nc.semaphore("SQ"))
    _HQ = es.enter_context(nc.semaphore("HQ"))
    _AQ = es.enter_context(nc.semaphore("AQ"))
    _AR = es.enter_context(nc.semaphore("AR"))
    block = es.enter_context(nc.Block())
    with es:
        PS = [ps0, ps1, ps2, ps3, ps4, ps5, ps6, ps7]
        XQ, WQ, XR, WR = Ctr(_XQ), Ctr(_WQ), Ctr(_XR), Ctr(_WR)
        VP, SP, TP = Ctr(_VP), Ctr(_SP), Ctr(_TP)
        PQ, QQ, LQ, GQ = Ctr(_PQ), Ctr(_QQ), Ctr(_LQ), Ctr(_GQ)
        SQ, HQ, AQ, AR = Ctr(_SQ), Ctr(_HQ), Ctr(_AQ), Ctr(_AR)

        KT = a_big[:, 0:ND * S].rearrange("p (c t) -> p c t", c=ND)
        QT = a_big[:, ND * S:2 * ND * S].rearrange("p (c t) -> p c t", c=ND)
        HB = a_big[:].rearrange("p (c t) -> p c t", c=NHF)
        VTM = a_vA[:].rearrange("p (k c) -> p k c", k=16)
        WO = a_vA[:, 0:ND * 1024].rearrange("p (c n) -> p c n", c=ND)
        W1P = a_vA[:, 8192:8192 + 2 * ND * 128].rearrange("p (b d t) -> p b d t", b=2, d=ND)
        W2C = a_vA[:, 10240:10240 + 2 * 512].rearrange("p (b n) -> p b n", b=2)
        ATN = a_aA[:, 0:16 * 640].rearrange("p (h c) -> p h c", h=16)
        WVH = a_aA[:, 0:ND * 1024].rearrange("p (d n) -> p d n", d=ND)
        WCH = a_aA[:, 8192:8448].rearrange("p (b t) -> p b t", b=2)
        XQS = a_aA[:, 0:ND * 1024].rearrange("p (c t) -> p c t", c=ND)
        CBC = a_aA[:, 8192:9216]
        XQT = a_aF[:, 3072:4096]
        XT = a_aF[:, 0:1024]
        SELD = a_aF[0:8, 2048:3072]
        WQF = a_aF[:, 2048:3072]
        HALF = a_aF[:, 0:1024].rearrange("p (b n) -> p b n", b=2)
        OPX = a_aF[:, 0:1024]
        TMP2 = a_aF[:, 1024:2048]
        STT = [s_st0, s_st1]

        def emit(eng, fn, waits=(), inc=None, k=1):
            wl = tuple((c.h, v) for (c, v) in waits)
            if not wl:
                wl = ((_TP, 0),)
            ip = None
            if inc is not None:
                inc.n += k
                ip = (inc.h, k)

            def g(e, fn=fn, wl=wl, ip=ip):
                for (h, v) in wl:
                    e.wait_ge(h, v)
                instr = fn(e)
                if ip is not None:
                    instr.then_inc(*ip)
            ops[eng].append(g)
            return inc.n if inc is not None else None

        def fc(e):
            e.dma_start(out=c_mask[:], in_=mk_h).then_inc(XQ.h, 16)
            e.dma_start(out=c_id[:], in_=id_h).then_inc(XQ.h, 16)
            e.dma_start(out=c_b1[:], in_=b1_h).then_inc(XQ.h, 16)
            e.dma_start(out=c_b2[:], in_=b2_h).then_inc(XQ.h, 16)
        ops["sync"].append(fc)
        XQ.n += 64
        xq_consts = XQ.n

        emit("vector", lambda e: e.memset(s_det[:, 0:1], 1000.25), inc=VP)
        emit("vector", lambda e: e.tensor_scalar(s_det[:, 0:1], s_det[:, 0:1], M64, M64,
                                                 OP.add, OP.subtract), waits=[(VP, VP.n)], inc=VP)
        emit("vector", lambda e: e.tensor_scalar(s_det[:, 1:2], s_det[:, 0:1], 500.0, None, OP.is_ge),
             waits=[(VP, VP.n)], inc=VP)
        emit("vector", lambda e: e.tensor_scalar(s_det[:, 0:1], s_det[:, 0:1], 500.0, None, OP.is_lt),
             inc=VP)
        emit("vector", lambda e: e.tensor_scalar(s_det[:, 1:2], s_det[:, 1:2], M64, None, OP.mult),
             waits=[(VP, VP.n)], inc=VP)
        emit("vector", lambda e: e.tensor_scalar(s_det[:, 0:1], s_det[:, 0:1], M32, None, OP.mult),
             inc=VP)
        emit("vector", lambda e: e.tensor_add(s_M[:], s_det[:, 0:1], s_det[:, 1:2]),
             waits=[(VP, VP.n)], inc=VP)
        emit("vector", lambda e: e.memset(c_ones[:], 1.0), inc=VP)
        emit("vector", lambda e: e.memset(c_onesb[:], 1.0), inc=VP)
        emit("vector", lambda e: e.memset(s_eps[:], EPSLN), inc=VP)
        ones_cols = VTM[:, :, :].rearrange("p k (h c) -> p k h c", h=16)[:, :, :, 64:65]
        emit("vector", lambda e: e.memset(ones_cols, 1.0), inc=VP)
        vp_pre = VP.n

        wtiles = []
        for i in range(NHF):
            wtiles.append(("w1", w1_h[128 * i:128 * (i + 1), :], i))
        for r in range(8):
            for c in range(4):
                wtiles.append(("w2", w2_h[128 * r:128 * (r + 1), 1024 * c:1024 * (c + 1)], (r, c)))
        wsc_ready = [0]
        wscr_done = [0]
        wt_free = [0]
        _p1 = {"gate": 0, "k": 0, "gp": [0, 0]}
        _p2 = {"gate": 0, "k": 0, "tr": 0, "ev": 0, "trp": [0, 0], "wrp": [0, 0]}

        def emit_wp1():
            if _p1["k"] >= len(wtiles):
                return
            k = _p1["k"]
            _p1["k"] += 1
            kind, apw, _ = wtiles[k]
            par = k % 2
            wbuf = WQF if par == 0 else XQT
            WQc = WQ if par == 0 else WR
            wq_c = emit("sync", lambda e, apw=apw, wbuf=wbuf: e.dma_start(
                out=wbuf[:], in_=apw), waits=[(SP, _p1["gp"][par])], inc=WQc, k=16)
            vpn = emit("scalar", lambda e, k=k, wbuf=wbuf: e.activation(
                wbuf[:], wbuf[:], AFT.Abs, bias=0.0, scale=1.0,
                accum_out=s_wred[:, k:k + 1]), waits=[(WQc, wq_c)], inc=SP)
            _p1["gp"][par] = vpn
            _p1["gate"] = vpn

        def emit_wscale():
            emit("vector", lambda e: e.reduce_sum(s_wsc[:, 4:5], s_wred[:, 0:32], AX.X),
                 waits=[(SP, _p1["gate"])], inc=VP)
            emit("vector", lambda e: e.reduce_sum(s_wsc[:, 5:6], s_wred[:, 32:64], AX.X), inc=VP)

            def ft(e):
                e.matmul(PS[0][0:1, 0:1], c_ones[:, 0:1], s_wsc[:, 4:5], start=True, stop=True)
                return e.matmul(PS[0][0:1, 1:2], c_ones[:, 0:1], s_wsc[:, 5:6], start=True, stop=True)
            emit("tensor", ft, waits=[(VP, VP.n)], inc=TP)
            emit("vector", lambda e: e.tensor_scalar(s_wsc[0:1, 0:2], PS[0][0:1, 0:2],
                                                     1.0 / (HF * D), EPSQ, OP.mult, OP.max),
                 waits=[(TP, TP.n)], inc=VP)
            emit("tensor", lambda e: e.matmul(PS[0][:, 4:6], c_ones[0:1, :], s_wsc[0:1, 0:2],
                                              start=True, stop=True), waits=[(VP, VP.n)], inc=TP)
            emit("vector", lambda e: e.tensor_copy(s_wsc[:, 0:2], PS[0][:, 4:6]),
                 waits=[(TP, TP.n)], inc=VP)
            emit("vector", lambda e: e.reciprocal(s_wsc[:, 2:4], s_wsc[:, 0:2]),
                 waits=[(VP, VP.n)], inc=VP)
            emit("vector", lambda e: e.tensor_scalar(s_wsc[:, 6:7], s_wsc[:, 1:2], 1.0 / QB, None,
                                                     OP.mult), waits=[(VP, VP.n)], inc=VP)
            wsc_ready[0] = VP.n

        def emit_wp2():
            if _p2["k"] >= len(wtiles):
                return
            k = _p2["k"]
            _p2["k"] += 1
            kind, apw, meta = wtiles[k]
            rcol = 2 if kind == "w1" else 3
            par = k % 2
            wbuf = WQF if par == 0 else TMP2
            obuf = s_otm if par == 0 else s_dump
            wdw = [(VP, wsc_ready[0])]
            if _p2["trp"][par]:
                wdw.append((TP, _p2["trp"][par]))
            if par == 1 and k == 1:
                wdw += [(TP, a_tr_last[0]), (SP, xnT_ready[0]), (VP, stats_ready[0])]
            WQc = WQ if par == 0 else WR
            wq_c = emit("sync", lambda e, apw=apw, wbuf=wbuf: e.dma_start(
                out=wbuf[:], in_=apw), waits=wdw, inc=WQc, k=16)
            emit("vector", lambda e, rcol=rcol, wbuf=wbuf: e.tensor_scalar(
                wbuf[:], wbuf[:], s_wsc[:, rcol:rcol + 1], s_M[:],
                OP.mult, OP.add), waits=[(WQc, wq_c)], inc=VP)
            emit("vector", lambda e, wbuf=wbuf: e.tensor_scalar(
                wbuf[:], wbuf[:], s_M[:], 1.0, OP.subtract, OP.min),
                waits=[(VP, VP.n)], inc=VP)
            vq = emit("vector", lambda e, wbuf=wbuf: e.tensor_scalar(
                wbuf[:], wbuf[:], -1.0, None, OP.max), waits=[(VP, VP.n)], inc=VP)

            wtr = [(VP, vq)]
            if _p2["ev"]:
                wtr.append((SP, _p2["ev"]))

            def ftr(e, wbuf=wbuf):
                r = None
                for d2 in range(ND):
                    r = e.transpose(PS[d2 // 4][:, 128 * (d2 % 4):128 * (d2 % 4 + 1)],
                                    wbuf[:, 128 * d2:128 * (d2 + 1)], c_id[:])
                return r
            tpw = emit("tensor", ftr, waits=wtr, inc=TP)
            _p2["trp"][par] = tpw
            _p2["tr"] = tpw

            def fev(e, obuf=obuf):
                r = None
                for d2 in range(ND):
                    r = e.activation(obuf[:, 128 * d2:128 * (d2 + 1)],
                                     PS[d2 // 4][:, 128 * (d2 % 4):128 * (d2 % 4 + 1)],
                                     AFT.Copy, bias=0.0, scale=1.0)
                return r
            sqw = ((16 * (k - 1) + 31) // 32) * 32 if k >= 2 else 0
            spe = emit("scalar", fev, waits=[(TP, tpw), (SQ, sqw)],
                       inc=SP, k=ND)
            _p2["ev"] = spe

            if kind == "w1":
                i = meta

                def fw1(e, i=i, spc=spe, obuf=obuf):
                    e.wait_ge(SP.h, spc)
                    e.dma_start(out=w1qt_h[i, :, :, :].rearrange("d p t -> p d t"),
                                in_=obuf[:].rearrange("p (d t) -> p d t", d=ND)
                                ).then_inc(SQ.h, 16)
                ops["scalar"].append(fw1)
                SQ.n += 16
            else:
                r_, c_ = meta

                def fw2(e, r_=r_, c_=c_, spc=spe, obuf=obuf):
                    e.wait_ge(SP.h, spc)
                    e.dma_start(out=w2qt_h[8 * c_:8 * c_ + 8, r_, :, :]
                                .rearrange("a p t -> p a t"),
                                in_=obuf[:].rearrange("p (a t) -> p a t", a=ND)
                                ).then_inc(SQ.h, 16)
                ops["scalar"].append(fw2)
                SQ.n += 16
            _p2["wrp"][par] = SQ.n
            wscr_done[0] = SQ.n

        stats_ready = [0] * NSEQ
        xnT_ready = [0] * NSEQ
        srow_ready = [0] * NSEQ
        kqv_ready = [0] * NSEQ
        oT_ready = [0] * NSEQ
        oT_tp = [0] * NSEQ
        a_tr_last = [0] * NSEQ
        aF_free = [0]
        aR_free = [0]
        xnT_free = [0]
        d_prev = {"vp": 0, "sp": 0, "tp": 0, "wq": 0}
        oevac_gate = {}

        def emit_A(s, wp1_slots=0):
            st = STT[s]
            tile_done = {}
            mrs_done = {}
            xtr_done = {}
            xdma_free = {}
            for i in range(NT):
                XTb = XT if i % 2 == 0 else TMP2
                row0 = 128 * (NT * s + i)
                base = 48 if i % 2 == 0 else 56
                csum, csq, cvar, crst, cmrs, cxs, ccc = (base + j for j in range(7))
                w = [(XQ, xq_consts)]
                if i >= 2:
                    w += [(VP, xdma_free[i - 2]), (TP, xtr_done[i - 2])]
                if s > 0:
                    w += [(AQ, aF_free[0]), (AR, aR_free[0])]
                XQc = XQ if i % 2 == 0 else XR
                xq_i = emit("sync", lambda e, row0=row0, XTb=XTb: e.dma_start(
                    out=XTb[:], in_=x_h[row0:row0 + 128, :]), waits=w, inc=XQc, k=16)
                wv = [(XQc, xq_i)]
                if i >= 2:
                    wv.append((SP, mrs_done[i - 2]))
                vp_sum = emit("vector", lambda e, csum=csum, XTb=XTb: e.reduce_sum(
                    st[:, csum:csum + 1], XTb[:], AX.X), waits=wv, inc=VP)
                ws = [(XQc, xq_i)]
                if i >= 2:
                    ws.append((VP, tile_done[i - 2]))
                emit("scalar", lambda e, csq=csq, XTb=XTb: e.activation(
                    s_dump[:, :], XTb[:], AFT.Square, bias=0.0, scale=1.0,
                    accum_out=st[:, csq:csq + 1]), waits=ws, inc=SP)
                emit("scalar", lambda e, csum=csum, cvar=cvar: e.activation(
                    st[:, cvar:cvar + 1], st[:, csum:csum + 1], AFT.Square, bias=0.0, scale=1.0 / D),
                    waits=[(VP, vp_sum)], inc=SP)
                sp_a = emit("scalar", lambda e, csum=csum, cmrs=cmrs: e.activation(
                    st[:, cmrs:cmrs + 1], st[:, csum:csum + 1], AFT.Copy, bias=0.0, scale=-1.0 / D),
                    inc=SP)
                vp_var = emit("vector", lambda e, csq=csq, cvar=cvar: e.scalar_tensor_tensor(
                    st[:, cvar:cvar + 1], st[:, csq:csq + 1], 1.0 / D, st[:, cvar:cvar + 1],
                    OP.mult, OP.subtract), waits=[(SP, sp_a)], inc=VP)
                sp_std = emit("scalar", lambda e, cvar=cvar, crst=crst: e.activation(
                    st[:, crst:crst + 1], st[:, cvar:cvar + 1], AFT.Sqrt, bias=s_eps[:], scale=1.0),
                    waits=[(VP, vp_var)], inc=SP)
                vp_rst = emit("vector", lambda e, crst=crst: e.reciprocal(
                    st[:, crst:crst + 1], st[:, crst:crst + 1]), waits=[(SP, sp_std)], inc=VP)
                sp_mrs = emit("scalar", lambda e, cmrs=cmrs, crst=crst: e.activation(
                    st[:, cmrs:cmrs + 1], st[:, cmrs:cmrs + 1], AFT.Copy, bias=0.0,
                    scale=st[:, crst:crst + 1]), waits=[(VP, vp_rst)], inc=SP)
                mrs_done[i] = sp_mrs
                vp_xn = emit("vector", lambda e, crst=crst, cmrs=cmrs, XTb=XTb: e.tensor_scalar(
                    XTb[:], XTb[:], st[:, crst:crst + 1], st[:, cmrs:cmrs + 1],
                    OP.mult, OP.add), waits=[(SP, sp_mrs)], inc=VP)
                vp_xs = emit("vector", lambda e, cxs=cxs, XTb=XTb: e.reduce_max(
                    st[:, cxs:cxs + 1], XTb[:], AX.X, apply_absolute_value=True),
                    waits=[(VP, vp_xn)], inc=VP)
                xdma_free[i] = vp_xs
                emit("vector", lambda e, cxs=cxs, i=i: e.tensor_scalar(
                    st[:, 16 + i:16 + i + 1], st[:, cxs:cxs + 1], EPSQ, 1.0 / QB, OP.max, OP.mult),
                    waits=[(VP, vp_xs)], inc=VP)
                vp_c = emit("vector", lambda e, i=i, ccc=ccc: e.reciprocal(
                    st[:, ccc:ccc + 1], st[:, 16 + i:16 + i + 1]), waits=[(VP, VP.n)], inc=VP)
                vp_rc = emit("vector", lambda e, i=i, crst=crst, ccc=ccc: e.tensor_scalar(
                    st[:, i:i + 1], st[:, crst:crst + 1], st[:, ccc:ccc + 1], None, OP.mult),
                    waits=[(VP, vp_c)], inc=VP)
                vp_mc = emit("vector", lambda e, i=i, cmrs=cmrs, ccc=ccc: e.tensor_scalar(
                    st[:, 32 + i:32 + i + 1], st[:, cmrs:cmrs + 1], st[:, ccc:ccc + 1], None,
                    OP.mult), waits=[(VP, vp_rc), (SP, sp_mrs)], inc=VP)
                tile_done[i] = vp_mc

                def ftr(e, XTb=XTb):
                    r = None
                    for d2 in range(ND):
                        r = e.transpose(PS[4 + d2 // 4][:, 128 * (d2 % 4):128 * (d2 % 4 + 1)],
                                        XTb[:, 128 * d2:128 * (d2 + 1)], c_id[:])
                    return r
                tp_x = emit("tensor", ftr, waits=[(VP, vp_xn)], inc=TP)
                xtr_done[i] = tp_x

                def fxe(e, i=i):
                    r = None
                    for d2 in range(ND):
                        r = e.activation(a_xnT[:, d2, 128 * i:128 * (i + 1)],
                                         PS[4 + d2 // 4][:, 128 * (d2 % 4):128 * (d2 % 4 + 1)],
                                         AFT.Copy, bias=0.0, scale=1.0)
                    return r
                xnev = emit("scalar", fxe, waits=[(TP, tp_x)], inc=SP, k=ND)
                for _ in range(wp1_slots):
                    emit_wp1()
            stats_ready[s] = VP.n
            xnT_ready[s] = xnev
            srow_ready[s] = VP.n
            a_tr_last[s] = xtr_done[NT - 1]

        def emit_B(s, wp1=False):
            gq_base = GQ.n
            chunk_done = {}
            evac_done = {}
            for pi, (wh, dstT) in enumerate(((wq_h, QT), (wk_h, KT))):
                for h8 in range(ND):
                    for d in range(ND):
                        ci = pi * 64 + h8 * ND + d
                        w = []
                        if ci >= 2:
                            w.append((TP, chunk_done[ci - 2]))
                        if s > 0 and ci == 0:
                            w.append((TP, d_prev["tp"]))
                        emit("gpsimd", lambda e, wh=wh, h8=h8, d=d, ci=ci: e.dma_start(
                            out=WCH[:, ci % 2, :],
                            in_=wh[128 * d:128 * (d + 1), 128 * h8:128 * (h8 + 1)]),
                            waits=w, inc=GQ, k=16)
                        wq_c = min(gq_base + 16 * (ci - ci % 2 + 2), gq_base + 16 * 128)
                        wt = [(GQ, wq_c)]
                        if d == 0:
                            wt.append((SP, xnT_ready[s]))
                            key = (pi, h8 - 1) if h8 > 0 else ((pi - 1, ND - 1) if pi > 0 else None)
                            if key is not None:
                                wt.append((VP, evac_done[key]))
                            if s > 0:
                                wt.append((VP, d_prev["vp"]))
                                wt.append((SP, d_prev["sp"]))

                        def fmm(e, ci=ci, d=d):
                            r = None
                            for st4 in range(4):
                                r = e.matmul(PS[4 + st4][:, :], WCH[:, ci % 2, :],
                                             a_xnT[:, d, 512 * st4:512 * (st4 + 1)],
                                             start=(d == 0), stop=(d == ND - 1))
                            return r
                        tpc = emit("tensor", fmm, waits=wt, inc=TP)
                        chunk_done[ci] = tpc
                        if wp1 and ci >= 32:
                            emit_wp2()

                    def fev(e, h8=h8, dstT=dstT):
                        r = None
                        for st4 in range(4):
                            r = e.tensor_copy(dstT[:, h8, 512 * st4:512 * (st4 + 1)], PS[4 + st4][:, :])
                        return r
                    wv = [(TP, TP.n)]
                    if s > 0:
                        wv.append((TP, d_prev["tp"]))
                    ev = emit("vector", fev, waits=wv, inc=VP, k=4)
                    evac_done[(pi, h8)] = ev
                    if wp1:
                        emit_wp1()
            w0 = [(XQ, xq_consts), (TP, chunk_done[127])]
            if wp1 and _p2["ev"]:
                w0.append((SP, _p2["ev"]))
            if s > 0:
                w0.append((VP, d_prev["vp"]))
                w0.append((TP, d_prev["tp"]))

            def fwv(e, wl=tuple((c.h, v) for (c, v) in w0 if v > 0)):
                for (hh, v) in wl:
                    e.wait_ge(hh, v)
                for d in range(ND):
                    e.dma_start(out=WVH[:, d, :], in_=wv_h[128 * d:128 * (d + 1), :]).then_inc(GQ.h, 16)
            ops["gpsimd"].append(fwv)
            GQ.n += ND * 16
            wq_v = GQ.n
            wones = [(VP, VP.n)]
            if s > 0:
                wones += [(VP, d_prev["vp"]), (TP, d_prev["tp"]), (SP, d_prev["sp"])]
            vp_ones = emit("vector", lambda e: e.memset(ones_cols, 1.0), waits=wones, inc=VP)
            v_evac = {}
            for kt in range(16):
                wt = [(GQ, wq_v), (VP, evac_done[(1, ND - 1)]), (VP, srow_ready[s])]
                if kt >= 2:
                    wt.append((VP, v_evac[kt - 2]))
                if s > 0 and kt == 0:
                    wt.append((VP, d_prev["vp"]))
                    wt.append((SP, d_prev["sp"]))

                def fmm(e, kt=kt):
                    r = None
                    for d in range(ND):
                        for half in range(2):
                            r = e.matmul(PS[(kt % 2) * 2 + half][:, :],
                                         a_xnT[:, d, 128 * kt:128 * (kt + 1)],
                                         WVH[:, d, 512 * half:512 * (half + 1)],
                                         start=(d == 0), stop=(d == ND - 1))
                    return r
                tpv = emit("tensor", fmm, waits=wt, inc=TP)

                def fev(e, kt=kt):
                    dst = VTM[:, kt, :].rearrange("p (h c) -> p h c", h=16)[:, :, 0:64]
                    e.tensor_copy(dst[:, 0:8, :],
                                  PS[(kt % 2) * 2][:, :].rearrange("p (h c) -> p h c", h=8))
                    return e.tensor_copy(dst[:, 8:16, :],
                                         PS[(kt % 2) * 2 + 1][:, :].rearrange("p (h c) -> p h c", h=8))
                ev = emit("vector", fev, waits=[(TP, tpv)], inc=VP, k=2)
                v_evac[kt] = ev
                if wp1:
                    emit_wp1()
            kqv_ready[s] = VP.n

        def emit_C(s, wp2_per_tile=0):
            vp_c_entry = VP.n
            otr_done = {}
            exp_done = {}
            mask_done = {}
            oevac_done = {}
            oevac_sp = {}
            atn_gate = {}
            for t in range(NT):
                lo = max(0, 128 * t - 256)
                hi = min(S, 128 * t + 384)
                nkt = (hi - lo) // 128
                kt0 = lo // 128
                j0 = kt0 - (t - 2)
                for hp in range(NHP):
                    par = hp % 2
                    for h01 in range(2):
                        h = 2 * hp + h01
                        pc = PS[(0 if par == 0 else 4) if h01 == 0 else (2 if par == 0 else 5)]
                        pc2 = PS[1 if h01 == 0 else 3][:, 128 * par:128 * (par + 1)]
                        rows = slice(64 * h01, 64 * h01 + 64)
                        w = []
                        if t == 0 and hp == 0 and h01 == 0:
                            w.append((VP, max(kqv_ready[s], vp_c_entry)))
                        if t == 0 and hp < 2 and _p2["ev"]:
                            w.append((SP, _p2["ev"]))
                        if hp >= 2:
                            w.append((SP, exp_done[(t, hp - 2, h01)]))
                        elif t > 0:
                            w.append((VP, oevac_done[t - 1]))

                        def fsc(e, t=t, hp=hp, rows=rows, nkt=nkt, kt0=kt0, pc=pc, pc2=pc2):
                            r = None
                            for j in range(nkt):
                                dst = pc[:, 128 * j:128 * (j + 1)] if j < 4 else pc2[:, 0:128]
                                r = e.matmul(dst, KT[rows, hp, 128 * (kt0 + j):128 * (kt0 + j + 1)],
                                             QT[rows, hp, 128 * t:128 * (t + 1)],
                                             start=True, stop=True)
                            return r
                        tpsc = emit("tensor", fsc, waits=w, inc=TP)
                        n1 = min(nkt, 4) * 128
                        wx = [(TP, tpsc)]
                        ag = atn_gate.get((t, h), 0)
                        if ag:
                            wx.append((TP, ag))

                        def fex(e, h=h, pc=pc, pc2=pc2, nkt=nkt, n1=n1):
                            r = e.activation(ATN[:, h, 0:n1], pc[:, 0:n1], AFT.Exp,
                                             bias=0.0, scale=0.125)
                            if nkt > 4:
                                r = e.activation(ATN[:, h, 512:640], pc2[:, 0:128], AFT.Exp,
                                                 bias=0.0, scale=0.125)
                            return r
                        spx = emit("scalar", fex, waits=wx, inc=SP)
                        exp_done[(t, hp, h01)] = spx
                        vpm = emit("vector", lambda e, h=h, nkt=nkt, j0=j0: e.tensor_tensor(
                            ATN[:, h, 0:128 * nkt], ATN[:, h, 0:128 * nkt],
                            c_mask[:, 128 * j0:128 * (j0 + nkt)], OP.mult),
                            waits=[(SP, spx)], inc=VP)
                        mask_done[(t, h)] = vpm
                wav = [(VP, max(mask_done[(t, h)] for h in range(16)))]
                if t > 0:
                    wav.append((VP, oevac_done[t - 1]))

                def fav(e, nkt=nkt, kt0=kt0):
                    r = None
                    for hp in range(NHP):
                        for h01 in range(2):
                            h = 2 * hp + h01
                            c0 = 256 * (hp % 2) + 65 * h01
                            dst = PS[hp // 2][:, c0:c0 + 65]
                            for j in range(nkt):
                                r = e.matmul(dst, ATN[:, h, 128 * j:128 * (j + 1)],
                                             VTM[:, kt0 + j, 65 * h:65 * h + 65],
                                             start=(j == 0), stop=(j == nkt - 1))
                    return r
                tpav = emit("tensor", fav, waits=wav, inc=TP)
                oevac_gate[t] = tpav
                for h in range(16):
                    atn_gate[(t + 1, h)] = tpav

                def fr(e):
                    r = None
                    for qq in range(4):
                        v2 = PS[qq][:, :].rearrange("p (a c) -> p a c", a=2)
                        r = e.reciprocal(s_r[:, 4 * qq + 0:4 * qq + 2],
                                         v2[:, :, 64:65].rearrange("p a c -> p (a c)"))
                        r = e.reciprocal(s_r[:, 4 * qq + 2:4 * qq + 4],
                                         v2[:, :, 129:130].rearrange("p a c -> p (a c)"))
                    return r
                vpr = emit("vector", fr, waits=[(TP, tpav)], inc=VP, k=8)

                def foe(e):
                    r = None
                    for hp in range(NHP):
                        for h01 in range(2):
                            h = 2 * hp + h01
                            c0 = 256 * (hp % 2) + 65 * h01
                            rcol = 4 * (hp // 2) + (hp % 2) + 2 * h01
                            r = e.tensor_scalar(XQT[:, 64 * h:64 * (h + 1)],
                                                PS[hp // 2][:, c0:c0 + 64],
                                                s_r[:, rcol:rcol + 1], None, OP.mult)
                    return r
                woe = [(VP, vpr)]
                if t >= 1:
                    woe.append((TP, otr_done[t - 1]))
                vpoe = emit("vector", foe, waits=woe, inc=VP, k=16)
                oevac_done[t] = vpoe

                def ftr(e):
                    r = None
                    for c in range(ND):
                        r = e.transpose(PS[6 + c // 4][:, 128 * (c % 4):128 * (c % 4 + 1)],
                                        XQT[:, 128 * c:128 * (c + 1)], c_id[:])
                    return r
                wtro = [(VP, vpoe)]
                if t >= 1:
                    wtro.append((SP, oevac_sp[t - 1]))
                tp_o = emit("tensor", ftr, waits=wtro, inc=TP)
                otr_done[t] = tp_o

                def foev(e, t=t):
                    r = None
                    for c in range(ND):
                        r = e.activation(a_oT[:, c, 128 * t:128 * (t + 1)],
                                         PS[6 + c // 4][:, 128 * (c % 4):128 * (c % 4 + 1)],
                                         AFT.Copy, bias=0.0, scale=1.0)
                    return r
                oevac_sp[t] = emit("scalar", foev, waits=[(TP, tp_o)], inc=SP, k=ND)
            oT_ready[s] = oevac_sp[NT - 1]
            oT_tp[s] = otr_done[NT - 1]

        def emit_D(s):
            st_ = STT[s]
            def fwo(e, need=oevac_gate[NT - 1]):
                e.wait_ge(TP.h, need)
                for c in range(ND):
                    e.dma_start(out=WO[:, c, :], in_=wo_h[128 * c:128 * (c + 1), :]).then_inc(GQ.h, 16)
            ops["gpsimd"].append(fwo)
            GQ.n += ND * 16
            wq_wo = GQ.n
            gelu_gate = [0]
            opj_copy = {}
            halfev_vp = {}
            half_written = {}
            for b in range(NBLK):
                if b == 0:
                    def gsel(e, need=max(sel_last_tp[0], oevac_gate[NT - 1], wp2_tp[0])):
                        e.wait_ge(TP.h, need)
                        e.dma_start(out=SELD, in_=sel_h[0:8, 0:1024]).then_inc(LQ.h, 16)
                    ops["sync"].append(gsel)
                    LQ.n += 16
                xq_p3 = {}
                xqtr = {}
                sp_ev = {}
                for j in range(NBT):
                    t = NBT * b + j
                    row0 = 128 * (NT * s + t)
                    wxd = [(VP, xq_p3.get(j - 1, d_xq_gate[0])), (TP, d_pmaxtr[0]),
                           (XQ, XQ.n), (AR, tmp2slot[1])]
                    if j >= 1:
                        wxd.append((TP, xqtr[j - 1]))
                    elif b == 0:
                        wxd.append((TP, oT_tp[s]))
                    else:
                        wxd.append((TP, xqtr_prev[0]))
                    xd = emit("sync", lambda e, row0=row0: e.dma_start(
                        out=XQT[:, :], in_=x_h[row0:row0 + 128, :]), waits=wxd, inc=XQ, k=16)
                    p1 = emit("vector", lambda e, t=t: e.tensor_scalar(
                        XQT[:, :], XQT[:, :], st_[:, t:t + 1], st_[:, 32 + t:33 + t],
                        OP.mult, OP.add), waits=[(XQ, xd), (VP, stats_ready[s])], inc=VP)
                    p2 = emit("vector", lambda e: e.tensor_scalar(
                        XQT[:, :], XQT[:, :], s_M[:], s_M[:], OP.add, OP.subtract),
                        waits=[(VP, p1)], inc=VP)
                    p3 = emit("vector", lambda e, t=t: e.tensor_scalar(
                        XQT[:, :], XQT[:, :], st_[:, 16 + t:17 + t], None, OP.mult),
                        waits=[(VP, p2)], inc=VP)
                    xq_p3[j] = p3

                    def ftq(e):
                        r = None
                        for c2 in range(ND):
                            r = e.transpose(PS[6 + c2 // 4][:, 128 * (c2 % 4):128 * (c2 % 4 + 1)],
                                            XQT[:, 128 * c2:128 * (c2 + 1)], c_id[:])
                        return r
                    wtq_ = [(VP, p3)]
                    if j >= 1:
                        wtq_.append((SP, sp_ev[j - 1]))
                    else:
                        wtq_.append((SP, max(wp2_sp[0], xq_sp_prev[0], oT_ready[s])))
                    tp_q = emit("tensor", ftq, waits=wtq_, inc=TP)
                    xqtr[j] = tp_q

                    def fqe(e, j=j):
                        r = None
                        for c2 in range(ND):
                            r = e.activation(XQS[:, c2, 128 * j:128 * (j + 1)],
                                             PS[6 + c2 // 4][:, 128 * (c2 % 4):128 * (c2 % 4 + 1)],
                                             AFT.Copy, bias=0.0, scale=1.0)
                        return r
                    xq_done_sp = emit("scalar", fqe, waits=[(TP, tp_q)], inc=SP, k=ND)
                    sp_ev[j] = xq_done_sp
                xqtr_prev[0] = xqtr[NBT - 1]
                xq_sp_prev[0] = xq_done_sp
                xq_done_tp = xqtr[NBT - 1]
                d_xq_gate[0] = xq_p3[NBT - 1]
                pq_base = PQ.n
                panel_done = {}
                sp_gelu = {}
                for hc in range(NHF):
                    w = [(VP, wsc_ready[0]), (SQ, wscr_done[0]), (TP, w1slot[hc % 2])]
                    if hc < 2:
                        w.append((TP, oevac_gate[NT - 1]))
                    emit("sync", lambda e, hc=hc: e.dma_start(
                        out=W1P[:, hc % 2, :, :],
                        in_=w1qt_h[hc, :, :, :].rearrange("d p t -> p d t")), waits=w, inc=PQ, k=16)
                    xq_p = min(pq_base + 16 * (hc - hc % 2 + 2), pq_base + 16 * NHF)
                    wt = [(PQ, xq_p)]
                    if hc == 0:
                        wt += [(SP, xq_done_sp), (TP, gelu_gate[0])]
                        if b > 0:
                            wt += [(SP, opj_sp_last[0])]
                    if hc >= 2:
                        wt.append((SP, sp_gelu[hc - 2]))

                    def fm1(e, hc=hc):
                        r = None
                        for half in range(2):
                            for d in range(ND):
                                r = e.matmul(PS[2 * (hc % 2) + half][:, :], W1P[:, hc % 2, d, :],
                                             XQS[:, d, 512 * half:512 * (half + 1)],
                                             start=(d == 0), stop=(d == ND - 1))
                        return r
                    tp1 = emit("tensor", fm1, waits=wt, inc=TP)
                    panel_done[hc] = tp1
                    w1slot[hc % 2] = tp1
                    d_ffn1_last[0] = tp1

                    def fg(e, hc=hc):
                        r = None
                        for half in range(2):
                            r = e.activation(HB[:, hc, 512 * half:512 * (half + 1)],
                                             PS[2 * (hc % 2) + half][:, :], AFT.Gelu,
                                             bias=c_b1[:, hc:hc + 1], scale=s_wsc[:, 0:1])
                        return r
                    spg = emit("scalar", fg, waits=[(TP, tp1)], inc=SP)
                    sp_gelu[hc] = spg
                    if hc == 1:
                        emit("vector", lambda e: e.tensor_tensor(
                            s_dump[:, :], HB[:, 0, :], HB[:, 1, :], OP.max),
                            waits=[(SP, spg)], inc=VP)
                        emit("vector", lambda e: e.tensor_tensor(
                            s_otm[:, :], HB[:, 0, :], HB[:, 1, :], OP.min),
                            waits=[(VP, VP.n)], inc=VP)
                    elif hc >= 2:
                        emit("vector", lambda e, hc=hc: e.tensor_tensor(
                            s_dump[:, :], s_dump[:, :], HB[:, hc, :], OP.max),
                            waits=[(SP, spg), (VP, VP.n)], inc=VP)
                        emit("vector", lambda e, hc=hc: e.tensor_tensor(
                            s_otm[:, :], s_otm[:, :], HB[:, hc, :], OP.min),
                            waits=[(VP, VP.n)], inc=VP)
                emit("vector", lambda e: e.tensor_scalar(
                    s_otm[:, :], s_otm[:, :], -1.0, None, OP.mult), waits=[(VP, VP.n)], inc=VP)
                emit("vector", lambda e: e.tensor_tensor(
                    XQT[:, :], s_dump[:, :], s_otm[:, :], OP.max),
                    waits=[(VP, VP.n), (AR, tmp2slot[1])], inc=VP)
                vp_tree = VP.n
                for j in range(NBT):
                    tpt = emit("tensor", lambda e, j=j: e.transpose(
                        PS[4 + (j % 2)][:, 0:128],
                        XQT[:, 128 * j:128 * (j + 1)], c_id[:]),
                        waits=[(VP, vp_tree)] + ([(VP, hst_read[0])] if j >= 2 else []), inc=TP)
                    d_pmaxtr[0] = tpt
                    vpr1 = emit("vector", lambda e, j=j: e.reduce_max(
                        s_hst[:, j:j + 1], PS[4 + (j % 2)][:, 0:128], AX.X,
                        apply_absolute_value=True), waits=[(TP, tpt)], inc=VP)
                    hst_read[0] = vpr1
                    emit("vector", lambda e, j=j: e.tensor_scalar(
                        s_hst[:, j:j + 1], s_hst[:, j:j + 1], EPSQ, None, OP.max),
                        waits=[(VP, VP.n)], inc=VP)
                    emit("vector", lambda e, j=j: e.tensor_scalar(
                        s_hst[:, 8 + j:9 + j], s_hst[:, j:j + 1], s_wsc[:, 6:7], None, OP.mult),
                        waits=[(VP, VP.n)], inc=VP)
                    emit("vector", lambda e, j=j: e.reciprocal(
                        s_hst[:, 16 + j:17 + j], s_hst[:, j:j + 1]), waits=[(VP, VP.n - 1)], inc=VP)
                    emit("vector", lambda e, j=j: e.tensor_scalar(
                        s_hst[:, 16 + j:17 + j], s_hst[:, 16 + j:17 + j], QB, None, OP.mult),
                        waits=[(VP, VP.n)], inc=VP)
                vp_hst = VP.n
                tph = emit("tensor", lambda e: e.transpose(PS[4][0:8, 0:128], s_hst[:, 16:24],
                                                           c_id[:]),
                           waits=[(VP, vp_hst)], inc=TP)
                vph = emit("vector", lambda e: e.tensor_copy(s_hrow[0:8, 0:128], PS[4][0:8, 0:128]),
                           waits=[(TP, tph)], inc=VP)

                if b > 0:
                    def gsel0(e, need=TP.n):
                        e.wait_ge(TP.h, need)
                        e.dma_start(out=SELD, in_=sel_h[0:8, 0:1024]).then_inc(LQ.h, 16)
                    ops["sync"].append(gsel0)
                    LQ.n += 16
                xq_sel0 = LQ.n

                def fmhb(e, vph=vph, xq_sel0=xq_sel0, hr=hst_read[0]):
                    e.wait_ge(LQ.h, xq_sel0)
                    e.wait_ge(VP.h, vph)
                    e.wait_ge(VP.h, hr)
                    r = None
                    for j in range(NBT):
                        r = e.matmul(PS[4 + j // 4][:, 128 * (j % 4):128 * (j % 4 + 1)],
                                     SELD[:, 128 * j:128 * (j + 1)], s_hrow[0:8, 0:128],
                                     start=True, stop=True)
                    return r
                tpm = emit("tensor", fmhb, waits=[], inc=TP)
                sel_last_tp[0] = tpm
                vphb = emit("vector", lambda e: e.tensor_copy(CBC.bitcast(BF16)[:, 0:512], PS[4][:, :]),
                            waits=[(TP, TP.n)], inc=VP)
                vphb = emit("vector", lambda e: e.tensor_copy(CBC.bitcast(BF16)[:, 512:1024],
                                                              PS[5][:, :]), inc=VP)
                d_cbc_free[0] = 0
                for hc in range(NHF):
                    emit("vector", lambda e, hc=hc: e.tensor_tensor(
                        HB[:, hc, :], HB[:, hc, :], CBC.bitcast(BF16)[:, :], OP.mult),
                        waits=[(VP, vphb)], inc=VP)
                    emit("vector", lambda e, hc=hc: e.tensor_scalar(
                        HB[:, hc, :], HB[:, hc, :], s_M[:], s_M[:], OP.add, OP.subtract),
                        waits=[(VP, VP.n)], inc=VP)
                vp_hq = VP.n
                d_cbc_free[0] = vp_hq
                for dmh in range(2):
                    qq_base = QQ.n
                    ch_done = {}
                    for hc in range(NHF):
                        w = [(SQ, wscr_done[0]), (TP, w2slot[hc % 2])]
                        if hc < 2:
                            w.append((TP, oevac_gate[NT - 1]))
                        emit("sync", lambda e, hc=hc, dmh=dmh: e.dma_start(
                            out=W2C[:, hc % 2, :].rearrange("p (a t) -> p a t", a=4),
                            in_=w2qt_h[hc, 4 * dmh:4 * dmh + 4, :, :].rearrange("a p t -> p a t")),
                            waits=w, inc=QQ, k=16)
                        xq2 = min(qq_base + 16 * (hc - hc % 2 + 2), qq_base + 16 * NHF)
                        wt = [(QQ, xq2)]
                        if hc == 0:
                            wt.append((VP, vp_hq))
                            if dmh == 1:
                                wt.append((VP, halfev_vp[(b, 0, NBT - 1)]))

                        def fm2(e, hc=hc):
                            r = None
                            for j in range(NBT):
                                r = e.matmul(PS[j][:, :], HB[:, hc, 128 * j:128 * (j + 1)],
                                             W2C[:, hc % 2, :], start=(hc == 0), stop=(hc == NHF - 1))
                            return r
                        tp2 = emit("tensor", fm2, waits=wt, inc=TP)
                        ch_done[hc] = tp2
                        w2slot[hc % 2] = tp2
                    tp_f2 = TP.n
                    for j in range(NBT):
                        t = NBT * b + j
                        row0 = 128 * (NT * s + t)
                        w = [(TP, tp_f2), (HQ, halfslot[j % 2]), (VP, d_opx[0])]
                        hv = emit("vector", lambda e, j=j: e.tensor_scalar(
                            HALF[:, j % 2, :], PS[j][:, :], s_hst[:, 8 + j:9 + j], None, OP.mult),
                            waits=w, inc=VP)
                        halfev_vp[(b, dmh, j)] = hv
                        hw = emit("gpsimd", lambda e, j=j, row0=row0, dmh=dmh: e.dma_start(
                            out=y_h[row0:row0 + 128, 512 * dmh:512 * (dmh + 1)],
                            in_=HALF[:, j % 2, :]), waits=[(VP, hv), (HQ, HQ.n)], inc=HQ, k=16)
                        half_written[(dmh, j)] = hw
                        halfslot[j % 2] = hw
                gelu_gate[0] = TP.n
                for j in range(NBT):
                    t = NBT * b + j
                    row0 = 128 * (NT * s + t)
                    pa, pb = PS[2 * (j % 2)], PS[2 * (j % 2) + 1]
                    w = [(GQ, wq_wo), (SP, oT_ready[s]),
                         (VP, halfev_vp[(b, 1, 2 * (j % 2))]),
                         (VP, halfev_vp[(b, 1, 2 * (j % 2) + 1)]), (XQ, xq_consts)]
                    if j >= 2:
                        w.append((SP, opj_copy[j - 2]))

                    def fop(e, t=t, pa=pa, pb=pb):
                        e.matmul(pa[:, :], c_onesb[0:1, :], c_b2[0:1, 0:512], start=True, stop=False)
                        e.matmul(pb[:, :], c_onesb[0:1, :], c_b2[0:1, 512:1024], start=True, stop=False)
                        r = None
                        for c in range(ND):
                            e.matmul(pa[:, :], a_oT[:, c, 128 * t:128 * (t + 1)],
                                     WO[:, c, 0:512], start=False, stop=(c == ND - 1))
                            r = e.matmul(pb[:, :], a_oT[:, c, 128 * t:128 * (t + 1)],
                                         WO[:, c, 512:1024], start=False, stop=(c == ND - 1))
                        return r
                    tpo = emit("tensor", fop, waits=w, inc=TP)

                    OB = TMP2 if j % 2 == 0 else XQT
                    AQc = AQ if j % 2 == 0 else AR

                    def fcp(e, pa=pa, pb=pb, OB=OB):
                        e.copy(OB[:, 0:512], pa[:, :])
                        return e.copy(OB[:, 512:1024], pb[:, :])
                    spc = emit("scalar", fcp, waits=[(TP, tpo), (AQc, tmp2slot[j % 2])],
                               inc=SP, k=2)
                    opj_copy[j] = spc
                    wyr = [(SP, spc), (HQ, HQ.n),
                           (VP, max(d_opx[0], halfev_vp[(b, 1, NBT - 1)])), (XQ, XQ.n)]
                    yr = emit("sync", lambda e, row0=row0: e.dma_start(
                        out=OPX, in_=y_h[row0:row0 + 128, :]), waits=wyr, inc=XQ, k=16)
                    a1 = emit("vector", lambda e, OB=OB: e.tensor_tensor(
                        OB[:, :], OB[:, :], OPX, OP.add),
                        waits=[(XQ, yr), (SP, spc)], inc=VP)
                    xr = emit("sync", lambda e, row0=row0: e.dma_start(
                        out=OPX, in_=x_h[row0:row0 + 128, :]), waits=[(VP, a1)], inc=XQ, k=16)
                    a2 = emit("vector", lambda e, OB=OB: e.tensor_tensor(
                        OB[:, :], OB[:, :], OPX, OP.add),
                        waits=[(XQ, xr), (VP, a1)], inc=VP)
                    d_opx[0] = a2
                    fin = emit("gpsimd", lambda e, row0=row0, OB=OB, AQc=AQc: e.dma_start(
                        out=y_h[row0:row0 + 128, :], in_=OB[:, :]),
                        waits=[(VP, a2), (AQc, AQc.n)], inc=AQc, k=16)
                    tmp2slot[j % 2] = fin
                opj_sp_last[0] = opj_copy[NBT - 1]
            d_prev["vp"], d_prev["sp"], d_prev["tp"], d_prev["wq"] = VP.n, SP.n, TP.n, AQ.n
            aF_free[0] = AQ.n
            aR_free[0] = AR.n

        selbuf_gate = {}
        d_xq_gate = [0]
        d_pmaxtr = [0]
        d_opx = [0]
        xqtr_prev = [0]
        xq_sp_prev = [0]
        d_ffn1_last = [0]
        w1slot = [0, 0]
        w2slot = [0, 0]
        halfslot = [0, 0]
        tmp2slot = [0, 0]
        sel_last_tp = [0]
        wp2_tp = [0]
        wp2_sp = [0]
        spb_free = [0]
        hst_read = [0]
        d_cbc_free = [0]
        opj_sp_last = [0]

        emit_A(0, wp1_slots=4)
        while _p1["k"] < len(wtiles):
            emit_wp1()
        emit_wscale()
        emit_B(0, wp1=True)
        emit_C(0, wp2_per_tile=0)
        while _p2["k"] < len(wtiles):
            emit_wp2()
        wp2_tp[0] = _p2["tr"]
        wp2_sp[0] = _p2["ev"]
        emit_D(0)
        emit_A(1)
        emit_B(1)
        emit_C(1)
        emit_D(1)

        finals = [(XQ.h, XQ.n), (WQ.h, WQ.n), (XR.h, XR.n), (WR.h, WR.n), (VP.h, VP.n),
                  (SP.h, SP.n), (TP.h, TP.n), (PQ.h, PQ.n), (QQ.h, QQ.n),
                  (LQ.h, LQ.n), (GQ.h, GQ.n), (SQ.h, SQ.n), (HQ.h, HQ.n),
                  (AQ.h, AQ.n), (AR.h, AR.n)]

        def fend(e, finals=tuple(finals)):
            for (h, v) in finals:
                e.wait_ge(h, v)
        ops["gpsimd"].append(fend)

        @block.sync
        def _(e):
            for f_ in ops["sync"]:
                f_(e)

        @block.scalar
        def _(e):
            for f_ in ops["scalar"]:
                f_(e)

        @block.vector
        def _(e):
            for f_ in ops["vector"]:
                f_(e)

        @block.tensor
        def _(e):
            for f_ in ops["tensor"]:
                f_(e)

        @block.gpsimd
        def _(e):
            for f_ in ops["gpsimd"]:
                f_(e)

    return nc


_NC_CACHE = [None]


def kernel(L0_attn_hidden_states, L0_attn_w_q, L0_attn_w_k, L0_attn_w_v, L0_attn_w_o,
           L0_ffn_W1, L0_ffn_b1, L0_ffn_W2, L0_ffn_b2):
    x = np.ascontiguousarray(np.asarray(L0_attn_hidden_states, dtype=np.float32))
    wq = np.ascontiguousarray(np.asarray(L0_attn_w_q, dtype=np.float32))
    wk = np.ascontiguousarray(np.asarray(L0_attn_w_k, dtype=np.float32))
    wv = np.ascontiguousarray(np.asarray(L0_attn_w_v, dtype=np.float32))
    wo = np.ascontiguousarray(np.asarray(L0_attn_w_o, dtype=np.float32))
    w1 = np.ascontiguousarray(np.asarray(L0_ffn_W1, dtype=np.float32))
    w2 = np.ascontiguousarray(np.asarray(L0_ffn_W2, dtype=np.float32))
    b1 = np.ascontiguousarray(np.asarray(L0_ffn_b1, dtype=np.float32))
    b2 = np.ascontiguousarray(np.asarray(L0_ffn_b2, dtype=np.float32))
    b1c = np.ascontiguousarray(b1.reshape(NHF, 128).T)
    b2r = np.ascontiguousarray(b2.reshape(1, D)).astype(ml_dtypes.bfloat16)
    mask01, ident, sel16 = make_consts()

    if _NC_CACHE[0] is None:
        _NC_CACHE[0] = build_nc()
    nc = _NC_CACHE[0]

    in_maps = []
    for c in range(NCORE):
        xs = np.ascontiguousarray(x[NSEQ * c:NSEQ * (c + 1)].reshape(NSEQ * S, D))
        in_maps.append({
            "x": xs, "wq": wq, "wk": wk, "wv": wv, "wo": wo,
            "w1": w1, "w2": w2, "b1c": b1c, "b2r": b2r,
            "mask01": mask01, "ident": ident, "sel16": sel16,
        })
    res = run_bass_kernel_spmd(nc, in_maps, core_ids=list(range(NCORE)))
    y0 = np.asarray(res.results[0]["y"])
    out = np.empty((B, S, D), y0.dtype)
    out[0:NSEQ] = y0.reshape(NSEQ, S, D)
    for c in range(1, NCORE):
        out[NSEQ * c:NSEQ * (c + 1)] = np.asarray(res.results[c]["y"]).reshape(NSEQ, S, D)
    return out
